# version 04
# baseline (speedup 1.0000x reference)
import numpy as np
from contextlib import ExitStack

import concourse.bass as bass
import concourse.mybir as mybir
from concourse.bass_utils import run_bass_kernel_spmd

F32 = mybir.dt.float32
BF16 = mybir.dt.bfloat16
AF = mybir.ActivationFunctionType
ALU = mybir.AluOpType
AX = mybir.AxisListType

NCORE = 8
T = 1024
NT = 8
D = 2048
DC = 16
H = 16
HD = 128
FE = 1408
NBRANCH = 10
FCH = [(0, 512), (512, 512), (1024, 384)]
NFI = 11
NDO = 4
LN_EPS = 1e-5
ATT_EPS = 1e-9
ATT_SCALE = float(HD) ** -0.5
RING = 5
RING_COLS = 2048

C_LNEPS, C_ATTEPS, C_LNS = 0, 1, 2
(ST_SUM, ST_SQ, ST_MEAN, ST_M2, ST_MM, ST_VAR, ST_LNV, ST_RSTD,
 ST_M1, ST_NM1, ST_SSUM, ST_RSUM, ST_SM1, ST_SM2) = range(14)
BSUM, BSQ, BM2, BMEAN, BMM, BVAR, BLNV, BRSTD = (16, 24, 32, 40, 48, 56, 64, 72)


class Ctr:
    def __init__(self, handle):
        self.h = handle
        self.n = 0


class Builder:

    def __init__(self, nc, sems):
        self.nc = nc
        self.q = {e: [] for e in ("sync", "vector", "scalar", "tensor", "gpsimd")}
        for name, ctr in sems.items():
            setattr(self, name, ctr)
        self.bank_w = [None] * 8
        self.bank_r = [None] * 8
        self.widx = 0
        self.wfree_ms = {}

    def emit(self, eng, fn):
        self.q[eng].append(fn)

    def wait(self, eng, ms):
        if ms is None:
            return
        ctr, val = ms
        assert val <= ctr.n, f"wait on future milestone {val} > {ctr.n}"
        if val > 0:
            self.q[eng].append(lambda e, c=ctr, v=val: e.wait_ge(c.h, v))

    def op(self, eng, fn, ms=None):
        if ms is None:
            self.q[eng].append(fn)
            return None
        ms.n += 1
        v = ms.n
        self.q[eng].append(lambda e, f=fn, c=ms: f(e).then_inc(c.h, 1))
        return (ms, v)

    def opw(self, eng, fn, ms):
        r = self.op(eng, fn, ms)
        self.wait(eng, r)
        return r

    def dma(self, out, in_, ctr, eng="gpsimd"):
        ctr.n += 16
        v = ctr.n
        self.q[eng].append(lambda e, o=out, i=in_, c=ctr: e.dma_start(
            out=o, in_=i).then_inc(c.h, 16))
        return (ctr, v)

    def wload(self, in_ap, ncols, wring):
        idx = self.widx
        self.widx += 1
        k = idx % RING
        slot = wring[k][:, 0:ncols]
        if idx >= RING:
            self.wait("gpsimd", self.wfree_ms.pop(idx - RING))
        dst = slot
        if len(in_ap.shape) == 3:
            dst = slot.rearrange("p (i f) -> p i f", i=in_ap.shape[1],
                                 f=in_ap.shape[2])
        ready = self.dma(dst, in_ap, self.s_wslot[k])
        return idx, slot, ready

    def wfree(self, idx, ms):
        assert idx not in self.wfree_ms
        self.wfree_ms[idx] = ms

    def mm(self, bank, out_ap, lhsT, rhs, start, stop, ms=None, first=False,
           sgc=False):
        if first and self.bank_r[bank] is not None:
            self.wait("tensor", self.bank_r[bank])
        r = self.op("tensor", lambda e, o=out_ap, l=lhsT, rr=rhs, st=start,
                    sp=stop, g=sgc: e.matmul(o, l, rr, start=st, stop=sp,
                                             skip_group_check=g), ms)
        if stop and ms is not None:
            self.bank_w[bank] = r
        return r

    def phase_wait(self, banks=range(8)):
        best = {}
        for b in banks:
            ms = self.bank_r[b]
            if ms is None:
                continue
            ctr, val = ms
            if val > best.get(id(ctr), (ctr, 0))[1]:
                best[id(ctr)] = (ctr, val)
        for ctr, val in best.values():
            self.wait("tensor", (ctr, val))

    def evac(self, eng, fn, bank, ms_ctr):
        assert self.bank_w[bank] is not None
        self.wait(eng, self.bank_w[bank])
        r = self.op(eng, fn, ms_ctr)
        self.bank_r[bank] = r
        return r


def build(fake_cc=False, taps=(), salt=0):
    nc = bass.Bass(trn_type="TRN2", target_bir_lowering=False,
                   detect_race_conditions=False)

    x_in = nc.dram_tensor("x_in", [T, D], F32, kind="ExternalInput").ap()
    wqkv = nc.dram_tensor("wqkv", [48 * 128, 2048], F32, kind="ExternalInput").ap()
    wout = nc.dram_tensor("wout", [64 * 128, 512], F32, kind="ExternalInput").ap()
    wr_h = nc.dram_tensor("wr", [128, 128], F32, kind="ExternalInput").ap()
    wg_h = nc.dram_tensor("wg", [NBRANCH * FE * D], F32, kind="ExternalInput").ap()
    wu_h = nc.dram_tensor("wu", [NBRANCH * FE * D], F32, kind="ExternalInput").ap()
    wd_h = nc.dram_tensor("wd", [NBRANCH * NDO * NFI * 128, 512], F32,
                          kind="ExternalInput").ap()
    idn_h = nc.dram_tensor("idn_in", [128, 128], BF16, kind="ExternalInput").ap()
    mask_h = nc.dram_tensor("mask_in", [128, 8], F32, kind="ExternalInput").ap()
    cvec_h = nc.dram_tensor("cvec_in", [128, 8], F32, kind="ExternalInput").ap()
    y_out = nc.dram_tensor("y_out", [T, D], F32, kind="ExternalOutput").ap()
    cc_in = nc.dram_tensor("cc_in", [128, 32], F32, kind="Internal")
    cc_out = nc.dram_tensor("cc_out", [128 * NCORE, 32], F32, kind="Internal")

    tap_outs = {}
    for tname in taps:
        shp = {"xnT": [128, DC * 1024], "amix": [128, H * 1024],
               "r1": [128, NT * 2048], "wf": [128, 64],
               "racc": [128, NT * 2048], "xn2T": [128, DC * 1024],
               "phik": [128, H * 1024], "kvt": [128, H * 1024],
               "Sp": [128, 32], "pref": [128, 32],
               "phiq": [128, H * 1024], "gs": [128, NT * FE],
               "ao": [128, NT * 2048]}[tname]
        tap_outs[tname] = nc.dram_tensor("tap_" + tname, shp, F32,
                                         kind="ExternalOutput").ap()

    sbuf = lambda name, shape, dt: nc.alloc_sbuf_tensor(name, shape, dt).ap()
    r_bf = sbuf("r_bf", [128, NT * 2048], BF16)
    arenaA = sbuf("arenaA", [128, 16384], BF16)
    zeros = sbuf("zeros", [128, 1024], BF16)
    idn = sbuf("idn", [128, 128], BF16)
    wr_sb = sbuf("wr_sb", [128, 128], BF16)
    mask = sbuf("mask", [128, 8], F32)
    cvec = sbuf("cvec", [128, 8], F32)
    S = sbuf("S", [128, 32], F32)
    Stmp = sbuf("Stmp", [128, 8], F32)
    gath = sbuf("gath", [128, 256], F32)
    prefix = sbuf("prefix", [128, 32], F32)
    wfr = sbuf("wfr", [128, 64], F32)
    stats = sbuf("stats", [128, 128], F32)
    xn_tmp = sbuf("xn_tmp", [128, 2048], BF16)
    wring = [sbuf(f"wring{k}", [128, RING_COLS], BF16) for k in range(RING)]

    region_h = nc.alloc_sbuf_tensor("region", [128, 55296], BF16)
    region_addr = nc.lookup_mls(region_h).memorylocations[0].addr
    at = lambda name, off, shape, dt: nc.alloc_sbuf_tensor_at(
        name, shape, dt, offset=region_addr + off).ap()
    phiq = at("phiq", 0, [128, H * 1024], BF16)
    phik = at("phik", 32768, [128, H * 1024], BF16)
    kv = at("kv", 65536, [128, H * 1024], BF16)
    scr0 = at("scr0", 98304, [128, 1024], F32)
    scr1 = at("scr1", 102400, [128, 1024], F32)
    scr2 = at("scr2", 106496, [128, 1024], F32)
    ao = at("ao", 0, [128, NT * 2048], BF16)
    racc = at("racc", 0, [128, NT * 2048], BF16)
    gs = at("gs", 32768, [128, NT * FE], BF16)
    uw = at("uw", 55296, [128, NT * FE], BF16)
    guT = at("guT", 77824, [128, NT * FE], BF16)
    gutmp = at("gutmp", 100352, [128, FE], BF16)
    lg = at("lg", 103168, [128, 64], F32)
    exs = at("exs", 103424, [128, 64], F32)
    scs = at("scs", 103680, [128, 64], F32)

    scr3 = sbuf("scr3", [128, 1024], F32)
    pb = [nc.alloc_psum_tensor(f"pb{k}", [128, 512], F32).ap() for k in range(8)]

    with ExitStack() as stack:
        mk = lambda name: Ctr(stack.enter_context(nc.semaphore(name)))
        sems = dict(s_x=mk("s_x"), s_wf=mk("s_wf"),
                    s_mm=mk("s_mm"), s_ve=mk("s_ve"), s_ac=mk("s_ac"),
                    s_cc=mk("s_cc"), s_gt=mk("s_gt"), s_out=mk("s_out"),
                    s_gp=mk("s_gp"))
        B = Builder(nc, sems)
        B.s_wslot = [mk(f"s_wslot{k}") for k in range(RING)]
        s_xt = [mk(f"s_xt{j}") for j in range(NT)]
        s_x, s_wf = B.s_x, B.s_wf
        s_mm, s_ve, s_ac = B.s_mm, B.s_ve, B.s_ac
        s_cc, s_gt, s_out = B.s_cc, B.s_gt, B.s_out
        s_gp = B.s_gp

        x_ms = []
        for j in range(NT):
            x_ms.append(B.dma(r_bf[:, j * 2048:(j + 1) * 2048],
                              x_in[j * 128:(j + 1) * 128, :], s_xt[j]))
        B.dma(idn[:], idn_h[:], s_x)
        B.dma(mask[:], mask_h[:], s_x)
        B.dma(cvec[:], cvec_h[:], s_x)
        B.dma(wr_sb[:], wr_h[:], s_x)
        LOADS_ALL = (s_x, s_x.n)

        def tap(name, src_ap, ready_ms):
            if name not in taps:
                return
            B.wait("gpsimd", ready_ms)
            tms = B.dma(tap_outs[name], src_ap, s_out)
            B.wait("vector", tms)
            B.wait("scalar", tms)
            B.wait("tensor", tms)

        for _ in range(salt):
            B.emit("sync", lambda e: e.nop())
        B.op("vector", lambda e: e.memset(zeros[:], 0.0))
        B.wait("scalar", LOADS_ALL)
        B.wait("tensor", LOADS_ALL)

        def ln_group(src_fn, prewait_v_fn=None, prewait_s_fn=None,
                     tiles=range(NT), skip_mean=False, mean_from_accum=False):
            tiles = list(tiles)
            t0, tn = tiles[0], tiles[-1]
            sl8 = slice(t0, tn + 1)
            n8 = tn + 1 - t0
            if mean_from_accum:
                for j in tiles:
                    if prewait_v_fn is not None:
                        B.wait("vector", prewait_v_fn(j))
                B.opw("vector", lambda e: e.reduce_sum(
                    stats[:, BSUM + t0:BSUM + t0 + n8],
                    stats[:, 96 + 4 * t0:96 + 4 * (t0 + n8)].rearrange(
                        "p (j n) -> p j n", j=n8, n=4), axis=AX.X), s_ve)
            elif not skip_mean:
                for j in tiles:
                    if prewait_v_fn is not None:
                        B.wait("vector", prewait_v_fn(j))
                    B.op("vector", lambda e, s=src_fn(j), jj=j: e.reduce_sum(
                        stats[:, BSUM + jj:BSUM + jj + 1], s, axis=AX.X),
                        s_ve if j == tn else None)
                B.wait("vector", (s_ve, s_ve.n))
            for j in tiles:
                if prewait_s_fn is not None:
                    B.wait("scalar", prewait_s_fn(j))
                B.op("scalar", lambda e, s=src_fn(j), jj=j: e.activation(
                    scr0[:, 0:1024], s[:, 0:1024], AF.Square,
                    accum_out=stats[:, BSQ + jj:BSQ + jj + 1]))
                B.op("scalar", lambda e, s=src_fn(j), jj=j: e.activation(
                    scr1[:, 0:1024], s[:, 1024:2048], AF.Square,
                    accum_out=stats[:, BM2 + jj:BM2 + jj + 1]),
                    s_ac if j == tn else None)
            a_sq = (s_ac, s_ac.n)
            B.wait("scalar", a_sq)
            if mean_from_accum or not skip_mean:
                B.opw("vector", lambda e: e.tensor_scalar(
                    stats[:, BMEAN + t0:BMEAN + t0 + n8],
                    stats[:, BSUM + t0:BSUM + t0 + n8],
                    1.0 / D, None, ALU.mult), s_ve)
            B.wait("vector", a_sq)
            B.opw("vector", lambda e: e.tensor_tensor(
                stats[:, BSQ + t0:BSQ + t0 + n8],
                stats[:, BSQ + t0:BSQ + t0 + n8],
                stats[:, BM2 + t0:BM2 + t0 + n8], ALU.add), s_ve)
            B.opw("vector", lambda e: e.tensor_scalar(
                stats[:, BM2 + t0:BM2 + t0 + n8],
                stats[:, BSQ + t0:BSQ + t0 + n8],
                1.0 / D, None, ALU.mult), s_ve)
            B.opw("vector", lambda e: e.tensor_tensor(
                stats[:, BMM + t0:BMM + t0 + n8],
                stats[:, BMEAN + t0:BMEAN + t0 + n8],
                stats[:, BMEAN + t0:BMEAN + t0 + n8], ALU.mult), s_ve)
            v_var = B.opw("vector", lambda e: e.tensor_tensor(
                stats[:, BVAR + t0:BVAR + t0 + n8],
                stats[:, BM2 + t0:BM2 + t0 + n8],
                stats[:, BMM + t0:BMM + t0 + n8], ALU.subtract), s_ve)
            B.wait("scalar", v_var)
            B.opw("scalar", lambda e: e.activation(
                stats[:, BLNV + t0:BLNV + t0 + n8],
                stats[:, BVAR + t0:BVAR + t0 + n8],
                AF.Ln, bias=cvec[:, C_LNEPS:C_LNEPS + 1]), s_ac)
            a_rstd = B.opw("scalar", lambda e: e.activation(
                stats[:, BRSTD + t0:BRSTD + t0 + n8],
                stats[:, BLNV + t0:BLNV + t0 + n8],
                AF.Exp, scale=-0.5), s_ac)
            B.wait("vector", a_rstd)
            return a_rstd

        def ln_normalize(j, src, war_pe=None):
            B.wait("vector", war_pe)
            return B.op("vector", lambda e, s=src, jj=j: e.tensor_scalar(
                xn_tmp[:], s, stats[:, BMEAN + jj:BMEAN + jj + 1],
                stats[:, BRSTD + jj:BRSTD + jj + 1],
                ALU.subtract, ALU.mult), s_ve)

        def transpose16(dstT, j, norm_ms):
            last = None
            last_mm = None
            for g in range(4):
                bank = 6 + (g & 1)
                for m in range(4):
                    i = 4 * g + m
                    if m == 0:
                        B.wait("tensor", norm_ms if g == 0 else None)
                    last_mm = B.mm(bank, pb[bank][:, m * 128:(m + 1) * 128],
                                   xn_tmp[:, i * 128:(i + 1) * 128], idn[:],
                                   start=(m == 0), stop=(m == 3),
                                   ms=s_mm if m == 3 else None, first=(m == 0),
                                   sgc=True)
                dst = dstT.rearrange(
                    "p (i jj f) -> p i jj f", i=DC, jj=NT)[
                    :, 4 * g:4 * g + 4, j, :]
                last = B.evac("vector", lambda e, d=dst, b=bank:
                              e.tensor_copy(d, pb[b][:, 0:512]), bank, s_ve)
            return last, last_mm

        prev_tr_mm = None
        for half in range(2):
            ln_group(lambda j: r_bf[:, j * 2048:(j + 1) * 2048],
                     prewait_v_fn=lambda j: x_ms[j],
                     prewait_s_fn=lambda j: x_ms[j],
                     tiles=range(half * 4, half * 4 + 4))
            for j in range(half * 4, half * 4 + 4):
                nm = ln_normalize(j, r_bf[:, j * 2048:(j + 1) * 2048],
                                  war_pe=prev_tr_mm)
                _, prev_tr_mm = transpose16(arenaA, j, nm)
        xnT_ready = (s_ve, s_ve.n)
        tap("xnT", arenaA[:], xnT_ready)

        prev_ek = None
        prev_sttk = None
        B.wait("tensor", xnT_ready)
        s_done_ms = None
        bank_of = {"k": (2, 3), "q": (0, 1), "v": (4, 5)}

        def qkv_load(namec, h):
            c3 = {"q": 0, "k": 1, "v": 2}[namec]
            blk = c3 * 16 + h
            return B.wload(wqkv[blk * 128:(blk + 1) * 128, :], 2048, wring)

        def qkv_chains(namec, h, preloaded=None):
            idx, slot, ready = preloaded if preloaded else qkv_load(namec, h)
            B.wait("tensor", ready)
            last_ms = None
            for tc in range(2):
                bank = bank_of[namec][tc]
                for i in range(DC):
                    last_ms = B.mm(
                        bank, pb[bank][:],
                        slot[:, i * 128:(i + 1) * 128],
                        arenaA[:, i * 1024 + tc * 512:i * 1024 + tc * 512 + 512],
                        start=(i == 0), stop=(i == DC - 1),
                        ms=s_mm if i == DC - 1 else None, first=(i == 0))
            B.wfree(idx, last_ms)

        for h in range(H):
            qkv_chains("k", h)
            qkv_chains("v", h)
            ek_ms = None
            sttk_ms = None
            for tc in range(2):
                kb, vb = bank_of["k"][tc], bank_of["v"][tc]
                co = h * 1024 + tc * 512
                sl = slice(tc * 512, tc * 512 + 512)
                B.wait("vector", B.bank_w[kb])
                B.wait("vector", prev_ek)
                t0 = B.op("vector", lambda e, b=kb, s=sl: e.tensor_scalar(
                    scr0[:, s], pb[b][:], 0.0, None, ALU.min), s_ve)
                B.wait("scalar", t0)
                B.wait("scalar", prev_sttk)
                ek_ms = B.op("scalar", lambda e, s=sl: e.activation(
                    scr1[:, s], scr0[:, s], AF.Exp), s_ac)
                B.wait("vector", ek_ms)
                sttk_ms = B.evac("vector", lambda e, b=kb, s=sl, c=co, t=tc:
                                 e.scalar_tensor_tensor(
                                     phik[:, c:c + 512], pb[b][:], 0.0,
                                     scr1[:, s], ALU.max, ALU.add,
                                     accum_out=Stmp[:, t:t + 1]), kb, s_ve)
                B.evac("vector", lambda e, b=vb, c=co, t=tc:
                       e.scalar_tensor_tensor(
                           kv[:, c:c + 512], pb[b][:], 0.0,
                           phik[:, c:c + 512], ALU.add, ALU.mult,
                           accum_out=Stmp[:, 2 + t:3 + t]), vb, s_ve)
            prev_ek = ek_ms
            prev_sttk = sttk_ms
            B.wait("vector", (s_ve, s_ve.n))
            B.op("vector", lambda e, hh=h: e.tensor_tensor(
                S[:, 16 + hh:17 + hh], Stmp[:, 0:1], Stmp[:, 1:2], ALU.add))
            s_done_ms = B.op("vector", lambda e, hh=h: e.tensor_tensor(
                S[:, hh:hh + 1], Stmp[:, 2:3], Stmp[:, 3:4], ALU.add), s_ve)

        q_pre = [qkv_load("q", h) for h in range(RING)]

        tap("phik", phik[:], s_done_ms)
        tap("kvt", kv[:], s_done_ms)
        tap("phiq", phiq[:], s_done_ms)
        tap("Sp", S[:], s_done_ms)

        if fake_cc:
            B.wait("vector", s_done_ms)
            prefix_ms = B.op("vector", lambda e: e.memset(prefix[:], 0.0), s_ve)
        else:
            B.wait("gpsimd", s_done_ms)
            st_ms = B.dma(cc_in[:, :], S[:], s_gt)
            B.wait("gpsimd", st_ms)
            s_cc.n += 1
            B.emit("gpsimd", lambda e: e.collective_compute(
                "AllGather", ALU.bypass,
                replica_groups=[list(range(NCORE))],
                ins=[cc_in.ap().opt()], outs=[cc_out.ap().opt()],
            ).then_inc(s_cc.h, 1))
            cc_done = (s_cc, s_cc.n)
            prefix_ms = "deferred"

        prev_eq = None
        prev_sttq = None
        for h in range(H):
            qkv_chains("q", h, preloaded=q_pre[h] if h < len(q_pre) else None)
            eq_ms = None
            sttq_ms = None
            for tc in range(2):
                qb = bank_of["q"][tc]
                co = h * 1024 + tc * 512
                sl = slice(tc * 512, tc * 512 + 512)
                B.wait("vector", B.bank_w[qb])
                B.wait("vector", prev_eq if prev_eq is not None else prev_ek)
                t0q = B.op("vector", lambda e, b=qb, s=sl: e.tensor_scalar(
                    scr0[:, s], pb[b][:], 0.0, None, ALU.min), s_ve)
                B.wait("scalar", t0q)
                B.wait("scalar", prev_sttq if prev_sttq is not None
                       else prev_sttk)
                eq_ms = B.op("scalar", lambda e, s=sl: e.activation(
                    scr1[:, s], scr0[:, s], AF.Exp), s_ac)
                B.wait("vector", eq_ms)
                sttq_ms = B.evac("vector", lambda e, b=qb, s=sl, c=co:
                                 e.scalar_tensor_tensor(
                                     phiq[:, c:c + 512], pb[b][:], 0.0,
                                     scr1[:, s], ALU.max, ALU.add), qb, s_ve)
            prev_eq = eq_ms
            prev_sttq = sttq_ms

        if not fake_cc:
            B.wait("gpsimd", cc_done)
            g_ms = None
            for rk in range(NCORE):
                g_ms = B.dma(gath[:, rk * 32:(rk + 1) * 32],
                             cc_out[rk * 128:(rk + 1) * 128, :], s_gt)
            B.wait("vector", g_ms)
            B.wait("vector", LOADS_ALL)
            B.opw("vector", lambda e: e.memset(prefix[:], 0.0), s_ve)
            prefix_ms = None
            for rk in range(NCORE):
                prefix_ms = B.opw(
                    "vector", lambda e, r=rk: e.scalar_tensor_tensor(
                        prefix[:], gath[:, r * 32:(r + 1) * 32],
                        mask[:, r:r + 1], prefix[:], ALU.mult, ALU.add), s_ve)

        tap("pref", prefix[:], prefix_ms)

        amix_ms = []
        prev_rec = None
        B.wait("vector", prefix_ms)
        B.wait("vector", (s_ac, s_ac.n))
        B.wait("gpsimd", prefix_ms)
        ao_evac = {}
        wout_free = []

        def passB_head(h):
            nonlocal prev_rec
            co = h * 1024
            B.wait("vector", prev_rec)
            B.op("vector", lambda e, c=co, hh=h: e.tensor_tensor_scan(
                scr0[:], kv[:, c:c + 1024], zeros[:],
                prefix[:, hh:hh + 1], ALU.add, ALU.add))
            B.op("vector", lambda e, c=co, hh=h: e.tensor_tensor_scan(
                scr2[:], phik[:, c:c + 1024], zeros[:],
                prefix[:, 16 + hh:17 + hh], ALU.add, ALU.add))
            B.op("vector", lambda e, c=co: e.tensor_tensor(
                scr3[:], scr0[:], phiq[:, c:c + 1024], ALU.mult))
            den_ms = B.op("vector", lambda e, c=co: e.tensor_tensor(
                scr1[:], scr2[:], phiq[:, c:c + 1024], ALU.mult), s_ve)
            B.wait("scalar", den_ms)
            B.op("scalar", lambda e: e.activation(
                scr2[:], scr1[:], AF.Ln, bias=cvec[:, C_ATTEPS:C_ATTEPS + 1]))
            rec_ms = B.op("scalar", lambda e: e.activation(
                scr1[:], scr2[:], AF.Exp, scale=-1.0,
                bias=cvec[:, C_LNS:C_LNS + 1]), s_ac)
            B.wait("vector", rec_ms)
            amix_ms.append(B.op("vector", lambda e, c=co: e.tensor_tensor(
                arenaA[:, c:c + 1024], scr3[:], scr1[:], ALU.mult), s_ve))
            prev_rec = rec_ms

        def wout_pair(n, h2):
            blk = n * 16 + h2 * 2
            idx, slot, ready = B.wload(
                wout[blk * 128:(blk + 2) * 128, :].rearrange(
                    "(i p) f -> p i f", i=2, p=128), 1024, wring)
            last_ms = None
            for hi in range(2):
                hh = h2 * 2 + hi
                if n == 0:
                    B.wait("tensor", amix_ms[hh])
                if hi == 0:
                    B.wait("tensor", ready)
                for j in range(NT):
                    want_ms = (hh == H - 1)
                    m = B.mm(j, pb[j][:],
                             arenaA[:, hh * 1024 + j * 128:
                                    hh * 1024 + j * 128 + 128],
                             slot[:, hi * 512:(hi + 1) * 512],
                             start=(hh == 0), stop=(hh == H - 1),
                             ms=s_mm if want_ms else
                             (s_wf if (j == NT - 1 and hi == 1
                                       and hh < H - 1) else None),
                             first=(hh == 0))
                    if j == NT - 1 and hi == 1:
                        last_ms = m
            B.wfree(idx, last_ms)

        for h in range(H):
            passB_head(h)
            if h % 2 == 1:
                wout_pair(0, h // 2)
        for j in range(NT):
            ao_evac[(0, j)] = B.evac(
                "scalar", lambda e, jj=j: e.activation(
                    ao[:, jj * 2048:jj * 2048 + 512],
                    pb[jj][:], AF.Copy,
                    accum_out=stats[:, 96 + jj * 4:97 + jj * 4]), j, s_ac)
        for n in range(1, NDO):
            for h2 in range(H // 2):
                wout_pair(n, h2)
            for j in range(NT):
                ao_evac[(n, j)] = B.evac(
                    "scalar", lambda e, jj=j, nn=n: e.activation(
                        ao[:, jj * 2048 + nn * 512:jj * 2048 + nn * 512 + 512],
                        pb[jj][:], AF.Copy,
                        accum_out=stats[:, 96 + jj * 4 + nn:
                                        97 + jj * 4 + nn]), j, s_ac)
        B.opw("vector", lambda e: e.tensor_copy(
            stats[:, 80:88], stats[:, BMEAN:BMEAN + 8]), s_ve)
        ln_group(lambda j: ao[:, j * 2048:(j + 1) * 2048],
                 prewait_v_fn=lambda j: ao_evac[(NDO - 1, j)],
                 prewait_s_fn=lambda j: ao_evac[(NDO - 1, j)],
                 mean_from_accum=True)
        r1_ms = None
        radd_ms = []
        for j in range(NT):
            ln_normalize(j, ao[:, j * 2048:(j + 1) * 2048])
            r1_ms = B.op("vector", lambda e, jj=j: e.tensor_tensor(
                r_bf[:, jj * 2048:(jj + 1) * 2048],
                r_bf[:, jj * 2048:(jj + 1) * 2048], xn_tmp[:], ALU.add), s_ve)
            radd_ms.append(r1_ms)
        tap("amix", arenaA[:], amix_ms[-1])
        tap("r1", r_bf[:], r1_ms)
        tap("ao", ao[:], r1_ms)

        mm_after_wout = (s_mm, s_mm.n)
        B.opw("vector", lambda e: e.tensor_copy(
            stats[:, BMEAN:BMEAN + 8], stats[:, 80:88]), s_ve)
        ln_group(lambda j: r_bf[:, j * 2048:(j + 1) * 2048],
                 prewait_s_fn=lambda j: radd_ms[j], skip_mean=True)
        B.wait("tensor", mm_after_wout)
        prev_tr_mm = None
        for j in range(NT):
            nm = ln_normalize(j, r_bf[:, j * 2048:(j + 1) * 2048],
                              war_pe=prev_tr_mm)
            _, prev_tr_mm = transpose16(arenaA, j, nm)
        xn2T_ready = (s_ve, s_ve.n)
        tap("xn2T", arenaA[:], xn2T_ready)

        B.wait("tensor", xn2T_ready)
        rt_ms = None
        for j in range(NT):
            for i in range(DC):
                rt_ms = B.mm(6, pb[6][:, j * 8:j * 8 + 8],
                             arenaA[:, i * 1024 + j * 128:i * 1024 + j * 128 + 128],
                             wr_sb[:, i * 8:(i + 1) * 8],
                             start=(j == 0 and i == 0),
                             stop=(j == NT - 1 and i == DC - 1),
                             ms=s_mm if (j == NT - 1 and i == DC - 1) else None,
                             first=(j == 0 and i == 0), sgc=True)
        lg_ms = B.evac("vector", lambda e: e.tensor_copy(lg[:, 0:64],
                                                         pb[6][:, 0:64]), 6, s_ve)
        for j in range(NT):
            sl = slice(8 * j, 8 * j + 8)
            B.opw("vector", lambda e, s=sl: e.reduce_max(
                stats[:, ST_M1:ST_M1 + 1], lg[:, s], axis=AX.X), s_ve)
            nm1 = B.opw("vector", lambda e: e.tensor_scalar(
                stats[:, ST_NM1:ST_NM1 + 1], stats[:, ST_M1:ST_M1 + 1],
                -1.0, None, ALU.mult), s_ve)
            B.wait("scalar", nm1)
            ex_ms = B.opw("scalar", lambda e, s=sl: e.activation(
                exs[:, s], lg[:, s], AF.Exp,
                bias=stats[:, ST_NM1:ST_NM1 + 1]), s_ac)
            B.wait("vector", ex_ms)
            B.opw("vector", lambda e, s=sl: e.reduce_sum(
                stats[:, ST_SSUM:ST_SSUM + 1], exs[:, s], axis=AX.X), s_ve)
            B.opw("vector", lambda e: e.reciprocal(
                stats[:, ST_RSUM:ST_RSUM + 1],
                stats[:, ST_SSUM:ST_SSUM + 1]), s_ve)
            B.opw("vector", lambda e, s=sl: e.tensor_scalar(
                scs[:, s], exs[:, s], stats[:, ST_RSUM:ST_RSUM + 1],
                None, ALU.mult), s_ve)
            B.opw("vector", lambda e, s=sl: e.reduce_max(
                stats[:, ST_SM1:ST_SM1 + 1], scs[:, s], axis=AX.X), s_ve)
            B.opw("vector", lambda e, s=sl: e.tensor_scalar(
                exs[:, s], scs[:, s], stats[:, ST_SM1:ST_SM1 + 1],
                None, ALU.is_equal), s_ve)
            B.opw("vector", lambda e, s=sl: e.tensor_tensor(
                lg[:, s], scs[:, s], exs[:, s], ALU.subtract), s_ve)
            B.opw("vector", lambda e, s=sl: e.reduce_max(
                stats[:, ST_SM2:ST_SM2 + 1], lg[:, s], axis=AX.X), s_ve)
            B.opw("vector", lambda e, s=sl: e.tensor_scalar(
                exs[:, s], scs[:, s], stats[:, ST_SM2:ST_SM2 + 1],
                None, ALU.is_ge), s_ve)
            B.opw("vector", lambda e, s=sl: e.tensor_tensor(
                wfr[:, s], scs[:, s], exs[:, s], ALU.mult), s_ve)
        wf_ms = (s_ve, s_ve.n)
        tap("wf", wfr[:], wf_ms)

        B.wait("scalar", wf_ms)
        racc_evac = {}
        for b in range(NBRANCH):
            is_sh = b < 2
            eidx = b - 2
            for mat, wh in (("g", wg_h), ("u", wu_h)):
                for ci, (fc0, wdt) in enumerate(FCH):
                    for i4 in range(4):
                        off = b * FE * D + fc0 * D + (i4 * 4) * 128 * wdt
                        qsl = wh[off:off + 4 * 128 * wdt].rearrange(
                            "(i p f) -> p i f", i=4, p=128, f=wdt)
                        idx, slot, ready = B.wload(qsl, 4 * wdt, wring)
                        B.wait("tensor", ready)
                        last_ms = None
                        for ii in range(4):
                            i = i4 * 4 + ii
                            for j in range(NT):
                                is_end = (i == DC - 1)
                                m = B.mm(
                                    j, pb[j][:, 0:wdt],
                                    arenaA[:, i * 1024 + j * 128:
                                           i * 1024 + j * 128 + 128],
                                    slot[:, ii * wdt:(ii + 1) * wdt],
                                    start=(i == 0), stop=is_end,
                                    ms=s_mm if is_end else
                                    (s_wf if (ii == 3 and j == NT - 1
                                              and i != DC - 1) else None),
                                    first=(i == 0))
                                if ii == 3 and j == NT - 1:
                                    last_ms = m
                        B.wfree(idx, last_ms)
                    for j in range(NT):
                        dsl = slice(j * FE + fc0, j * FE + fc0 + wdt)
                        if mat == "g":
                            B.evac("scalar", lambda e, d=dsl, jj=j, w=wdt:
                                   e.activation(gs[:, d], pb[jj][:, 0:w],
                                                AF.Silu), j, s_ac)
                        else:
                            if is_sh:
                                B.evac("scalar", lambda e, d=dsl, jj=j, w=wdt:
                                       e.activation(uw[:, d], pb[jj][:, 0:w],
                                                    AF.Copy), j, s_ac)
                            else:
                                B.evac("scalar", lambda e, d=dsl, jj=j, w=wdt,
                                       ee=eidx: e.activation(
                                           uw[:, d], pb[jj][:, 0:w], AF.Copy,
                                           scale=wfr[:, 8 * jj + ee:
                                                     8 * jj + ee + 1]),
                                       j, s_ac)
            guT_ms = {}
            prev_tr = None
            for j in range(NT):
                B.wait("vector", B.bank_r[j])
                B.wait("vector", prev_tr)
                gu_ms = B.op("vector", lambda e, jj=j: e.tensor_tensor(
                    gutmp[:], gs[:, jj * FE:(jj + 1) * FE],
                    uw[:, jj * FE:(jj + 1) * FE], ALU.mult), s_ve)
                last_ev = None
                for g in range(3):
                    wdt = 4 if g < 2 else 3
                    bank = 6 + (g & 1)
                    for m in range(wdt):
                        fi = 4 * g + m
                        if m == 0 and g == 0:
                            B.wait("tensor", gu_ms)
                        tr = B.mm(bank, pb[bank][:, m * 128:(m + 1) * 128],
                                  gutmp[:, fi * 128:(fi + 1) * 128], idn[:],
                                  start=(m == 0), stop=(m == wdt - 1),
                                  ms=s_mm if m == wdt - 1 else None,
                                  first=(m == 0), sgc=True)
                        if g == 2 and m == wdt - 1:
                            prev_tr = tr
                    last_ev = B.evac(
                        "vector", lambda e, jj=j, gg=g, b=bank, w=wdt:
                        e.tensor_copy(
                            guT[:, jj * FE + gg * 512:jj * FE + gg * 512 + w * 128],
                            pb[b][:, 0:w * 128]), bank, s_ve)
                guT_ms[j] = last_ev
            for n in range(NDO):
                for f4 in range(3):
                    nfi = 4 if f4 < 2 else 3
                    rb = ((b * NDO + n) * NFI + f4 * 4) * 128
                    idx, slot, ready = B.wload(
                        wd_h[rb:rb + nfi * 128, :].rearrange(
                            "(i p) f -> p i f", i=nfi, p=128),
                        nfi * 512, wring)
                    B.wait("tensor", ready)
                    last_ms = None
                    for ii in range(nfi):
                        fi = f4 * 4 + ii
                        for j in range(NT):
                            if fi == 0 and n == 0:
                                B.wait("tensor", guT_ms[j])
                            is_end = (fi == NFI - 1)
                            m = B.mm(
                                j, pb[j][:],
                                guT[:, j * FE + fi * 128:j * FE + fi * 128 + 128],
                                slot[:, ii * 512:(ii + 1) * 512],
                                start=(fi == 0), stop=is_end,
                                ms=s_mm if is_end else
                                (s_wf if (ii == nfi - 1 and j == NT - 1
                                          and fi != NFI - 1) else None),
                                first=(fi == 0))
                            if ii == nfi - 1 and j == NT - 1:
                                last_ms = m
                    B.wfree(idx, last_ms)
                for j in range(NT):
                    dsl = slice(j * 2048 + n * 512, j * 2048 + n * 512 + 512)
                    if b == 0:
                        racc_evac[(n, j)] = B.evac(
                            "vector", lambda e, d=dsl, jj=j:
                            e.tensor_copy(racc[:, d], pb[jj][:]), j, s_ve)
                    elif b < NBRANCH - 1:
                        racc_evac[(n, j)] = B.evac(
                            "vector", lambda e, d=dsl, jj=j: e.tensor_tensor(
                                racc[:, d], racc[:, d], pb[jj][:], ALU.add),
                            j, s_ve)
                    else:
                        racc_evac[(n, j)] = B.evac(
                            "vector", lambda e, d=dsl, jj=j, nn=n:
                            e.scalar_tensor_tensor(
                                racc[:, d], pb[jj][:], 0.0, racc[:, d],
                                ALU.add, ALU.add,
                                accum_out=stats[:, 96 + jj * 4 + nn:
                                                97 + jj * 4 + nn]), j, s_ve)
        tap("racc", racc[:], racc_evac[(NDO - 1, NT - 1)])

        for half in range(2):
            ln_group(lambda j: racc[:, j * 2048:(j + 1) * 2048],
                     prewait_v_fn=lambda j: racc_evac[(NDO - 1, j)],
                     prewait_s_fn=lambda j: racc_evac[(NDO - 1, j)],
                     tiles=range(half * 4, half * 4 + 4),
                     mean_from_accum=True)
            for j in range(half * 4, half * 4 + 4):
                ln_normalize(j, racc[:, j * 2048:(j + 1) * 2048])
                add_ms = B.op("vector", lambda e, jj=j: e.tensor_tensor(
                    r_bf[:, jj * 2048:(jj + 1) * 2048],
                    r_bf[:, jj * 2048:(jj + 1) * 2048], xn_tmp[:],
                    ALU.add), s_ve)
                B.wait("gpsimd", add_ms)
                B.dma(y_out[j * 128:(j + 1) * 128, :],
                      r_bf[:, j * 2048:(j + 1) * 2048], s_out)
        B.wait("gpsimd", (s_out, s_out.n))

        with nc.Block() as block:
            @block.sync
            def _(eng):
                for f in B.q["sync"]:
                    f(eng)

            @block.vector
            def _(eng):
                for f in B.q["vector"]:
                    f(eng)

            @block.scalar
            def _(eng):
                for f in B.q["scalar"]:
                    f(eng)

            @block.tensor
            def _(eng):
                for f in B.q["tensor"]:
                    f(eng)

            @block.gpsimd
            def _(eng):
                for f in B.q["gpsimd"]:
                    f(eng)

    return nc



def _prep_inputs(L0_attn_x, L0_attn_W_qkv, L0_attn_W_out, L0_ffn_Wr, L0_ffn_Wg,
                 L0_ffn_Wu, L0_ffn_Wd, L0_ffn_Wsg, L0_ffn_Wsu, L0_ffn_Wsd):
    f32 = np.float32
    asc = np.ascontiguousarray
    x = np.asarray(L0_attn_x, f32).reshape(8192, D)

    Wqkv = np.asarray(L0_attn_W_qkv, f32)
    wqkv_host = asc(Wqkv.reshape(3, 16, 128, 16, 128)
                    .transpose(0, 1, 4, 3, 2)).reshape(48 * 128, 2048)
    Wo = np.asarray(L0_attn_W_out, f32)
    wout_host = asc(Wo.reshape(4, 512, 16, 128)
                    .transpose(0, 2, 3, 1)).reshape(64 * 128, 512)
    Wr = np.asarray(L0_ffn_Wr, f32)
    wr_host = asc(Wr.reshape(8, 16, 128).transpose(2, 1, 0)).reshape(128, 128)

    Wsg = np.asarray(L0_ffn_Wsg, f32)
    Wsu = np.asarray(L0_ffn_Wsu, f32)
    Wsd = np.asarray(L0_ffn_Wsd, f32)
    Wg = np.asarray(L0_ffn_Wg, f32)
    Wu = np.asarray(L0_ffn_Wu, f32)
    Wd = np.asarray(L0_ffn_Wd, f32)

    up_g = [Wsg[0:FE], Wsg[FE:2 * FE]] + [Wg[e] for e in range(8)]
    up_u = [Wsu[0:FE], Wsu[FE:2 * FE]] + [Wu[e] for e in range(8)]
    dn = [Wsd[:, 0:FE], Wsd[:, FE:2 * FE]] + [Wd[e] for e in range(8)]

    def up_flat(mats):
        parts = []
        for M in mats:
            for fc0, w in FCH:
                blk = M[fc0:fc0 + w].reshape(w, 16, 128).transpose(1, 2, 0)
                parts.append(asc(blk).reshape(-1))
        return np.concatenate(parts)

    wg_host = up_flat(up_g)
    wu_host = up_flat(up_u)

    dparts = []
    for M in dn:
        for n in range(NDO):
            for fi in range(NFI):
                blk = M[n * 512:(n + 1) * 512, fi * 128:(fi + 1) * 128].T
                dparts.append(asc(blk))
    wd_host = np.concatenate(dparts, axis=0)

    import ml_dtypes
    idn = np.eye(128, dtype=ml_dtypes.bfloat16)
    cvec = np.zeros((128, 8), f32)
    cvec[:, C_LNEPS] = LN_EPS
    cvec[:, C_ATTEPS] = ATT_EPS
    cvec[:, C_LNS] = np.log(ATT_SCALE)

    in_maps = []
    for c in range(NCORE):
        m = np.zeros((128, 8), f32)
        m[:, :c] = 1.0
        in_maps.append(dict(
            x_in=asc(x[c * T:(c + 1) * T]), wqkv=wqkv_host, wout=wout_host,
            wr=wr_host, wg=wg_host, wu=wu_host, wd=wd_host, idn_in=idn,
            mask_in=m, cvec_in=cvec))
    return in_maps


def kernel(L0_attn_x, L0_attn_W_qkv, L0_attn_W_out, L0_ffn_Wr, L0_ffn_Wg,
           L0_ffn_Wu, L0_ffn_Wd, L0_ffn_Wsg, L0_ffn_Wsu, L0_ffn_Wsd):
    in_maps = _prep_inputs(L0_attn_x, L0_attn_W_qkv, L0_attn_W_out, L0_ffn_Wr,
                           L0_ffn_Wg, L0_ffn_Wu, L0_ffn_Wd, L0_ffn_Wsg,
                           L0_ffn_Wsu, L0_ffn_Wsd)
    nc = build()
    res = run_bass_kernel_spmd(nc, in_maps, core_ids=list(range(NCORE)))
    out = np.concatenate(
        [np.asarray(res.results[c]["y_out"], np.float32) for c in range(NCORE)],
        axis=0)
    return out.reshape(1, 8192, D)


# version 05
# speedup vs baseline: 1.1414x; 1.1414x over previous
import numpy as np
from contextlib import ExitStack

import concourse.bass as bass
import concourse.mybir as mybir
from concourse.bass_utils import run_bass_kernel_spmd

F32 = mybir.dt.float32
BF16 = mybir.dt.bfloat16
AF = mybir.ActivationFunctionType
ALU = mybir.AluOpType
AX = mybir.AxisListType

NCORE = 8
T = 1024
NT = 8
D = 2048
DC = 16
H = 16
HD = 128
FE = 1408
NBRANCH = 10
FCH = [(0, 512), (512, 512), (1024, 384)]
NFI = 11
NDO = 4
LN_EPS = 1e-5
ATT_EPS = 1e-9
ATT_SCALE = float(HD) ** -0.5
RING = 5
RING_COLS = 2048

C_LNEPS, C_ATTEPS, C_LNS = 0, 1, 2
(ST_SUM, ST_SQ, ST_MEAN, ST_M2, ST_MM, ST_VAR, ST_LNV, ST_RSTD,
 ST_M1, ST_NM1, ST_SSUM, ST_RSUM, ST_SM1, ST_SM2) = range(14)
BSUM, BSQ, BM2, BMEAN, BMM, BVAR, BLNV, BRSTD = (16, 24, 32, 40, 48, 56, 64, 72)


class Ctr:
    def __init__(self, handle):
        self.h = handle
        self.n = 0


class Builder:

    def __init__(self, nc, sems):
        self.nc = nc
        self.q = {e: [] for e in ("sync", "vector", "scalar", "tensor", "gpsimd")}
        for name, ctr in sems.items():
            setattr(self, name, ctr)
        self.bank_w = [None] * 8
        self.bank_r = [None] * 8
        self.widx = 0
        self.wfree_ms = {}

    def emit(self, eng, fn):
        self.q[eng].append(fn)

    def wait(self, eng, ms):
        if ms is None:
            return
        ctr, val = ms
        assert val <= ctr.n, f"wait on future milestone {val} > {ctr.n}"
        if val > 0:
            self.q[eng].append(lambda e, c=ctr, v=val: e.wait_ge(c.h, v))

    def op(self, eng, fn, ms=None):
        if ms is None:
            self.q[eng].append(fn)
            return None
        ms.n += 1
        v = ms.n
        self.q[eng].append(lambda e, f=fn, c=ms: f(e).then_inc(c.h, 1))
        return (ms, v)

    def opw(self, eng, fn, ms):
        r = self.op(eng, fn, ms)
        self.wait(eng, r)
        return r

    def dma(self, out, in_, ctr, eng="gpsimd"):
        ctr.n += 16
        v = ctr.n
        self.q[eng].append(lambda e, o=out, i=in_, c=ctr: e.dma_start(
            out=o, in_=i).then_inc(c.h, 16))
        return (ctr, v)

    def wload(self, in_ap, ncols, wring):
        idx = self.widx
        self.widx += 1
        k = idx % RING
        slot = wring[k][:, 0:ncols]
        if idx >= RING:
            self.wait("gpsimd", self.wfree_ms.pop(idx - RING))
        dst = slot
        if len(in_ap.shape) == 3:
            dst = slot.rearrange("p (i f) -> p i f", i=in_ap.shape[1],
                                 f=in_ap.shape[2])
        ready = self.dma(dst, in_ap, self.s_wslot[k])
        return idx, slot, ready

    def wfree(self, idx, ms):
        assert idx not in self.wfree_ms
        self.wfree_ms[idx] = ms

    def mm(self, bank, out_ap, lhsT, rhs, start, stop, ms=None, first=False,
           sgc=False):
        if first and self.bank_r[bank] is not None:
            self.wait("tensor", self.bank_r[bank])
        r = self.op("tensor", lambda e, o=out_ap, l=lhsT, rr=rhs, st=start,
                    sp=stop, g=sgc: e.matmul(o, l, rr, start=st, stop=sp,
                                             skip_group_check=g), ms)
        if stop and ms is not None:
            self.bank_w[bank] = r
        return r

    def phase_wait(self, banks=range(8)):
        best = {}
        for b in banks:
            ms = self.bank_r[b]
            if ms is None:
                continue
            ctr, val = ms
            if val > best.get(id(ctr), (ctr, 0))[1]:
                best[id(ctr)] = (ctr, val)
        for ctr, val in best.values():
            self.wait("tensor", (ctr, val))

    def evac(self, eng, fn, bank, ms_ctr):
        assert self.bank_w[bank] is not None
        self.wait(eng, self.bank_w[bank])
        r = self.op(eng, fn, ms_ctr)
        self.bank_r[bank] = r
        return r


def build(fake_cc=False, taps=(), salt=0):
    nc = bass.Bass(trn_type="TRN2", target_bir_lowering=False,
                   detect_race_conditions=False)

    x_in = nc.dram_tensor("x_in", [T, D], F32, kind="ExternalInput").ap()
    wqkv = nc.dram_tensor("wqkv", [48 * 128, 2048], F32, kind="ExternalInput").ap()
    wout = nc.dram_tensor("wout", [64 * 128, 512], F32, kind="ExternalInput").ap()
    wr_h = nc.dram_tensor("wr", [128, 128], F32, kind="ExternalInput").ap()
    wg_h = nc.dram_tensor("wg", [NBRANCH * FE * D], F32, kind="ExternalInput").ap()
    wu_h = nc.dram_tensor("wu", [NBRANCH * FE * D], F32, kind="ExternalInput").ap()
    wd_h = nc.dram_tensor("wd", [NBRANCH * NDO * NFI * 128, 512], F32,
                          kind="ExternalInput").ap()
    idn_h = nc.dram_tensor("idn_in", [128, 128], BF16, kind="ExternalInput").ap()
    mask_h = nc.dram_tensor("mask_in", [128, 8], F32, kind="ExternalInput").ap()
    cvec_h = nc.dram_tensor("cvec_in", [128, 8], F32, kind="ExternalInput").ap()
    y_out = nc.dram_tensor("y_out", [T, D], F32, kind="ExternalOutput").ap()
    cc_in = nc.dram_tensor("cc_in", [128, 32], F32, kind="Internal")
    cc_out = nc.dram_tensor("cc_out", [128 * NCORE, 32], F32, kind="Internal")

    tap_outs = {}
    for tname in taps:
        shp = {"xnT": [128, DC * 1024], "amix": [128, H * 1024],
               "r1": [128, NT * 2048], "wf": [128, 64],
               "racc": [128, NT * 2048], "xn2T": [128, DC * 1024],
               "phik": [128, H * 1024], "kvt": [128, H * 1024],
               "Sp": [128, 32], "pref": [128, 32],
               "phiq": [128, H * 1024], "gs": [128, NT * FE],
               "ao": [128, NT * 2048]}[tname]
        tap_outs[tname] = nc.dram_tensor("tap_" + tname, shp, F32,
                                         kind="ExternalOutput").ap()

    sbuf = lambda name, shape, dt: nc.alloc_sbuf_tensor(name, shape, dt).ap()
    r_bf = sbuf("r_bf", [128, NT * 2048], BF16)
    arenaA = sbuf("arenaA", [128, 16384], BF16)
    zeros = sbuf("zeros", [128, 1024], BF16)
    idn = sbuf("idn", [128, 128], BF16)
    wr_sb = sbuf("wr_sb", [128, 128], BF16)
    mask = sbuf("mask", [128, 8], F32)
    cvec = sbuf("cvec", [128, 8], F32)
    S = sbuf("S", [128, 32], F32)
    Stmp = sbuf("Stmp", [128, 8], F32)
    gath = sbuf("gath", [128, 256], F32)
    prefix = sbuf("prefix", [128, 32], F32)
    wfr = sbuf("wfr", [128, 64], F32)
    stats = sbuf("stats", [128, 128], F32)
    xn_tmp = sbuf("xn_tmp", [128, 2048], BF16)
    wring = [sbuf(f"wring{k}", [128, RING_COLS], BF16) for k in range(RING)]

    region_h = nc.alloc_sbuf_tensor("region", [128, 55296], BF16)
    region_addr = nc.lookup_mls(region_h).memorylocations[0].addr
    at = lambda name, off, shape, dt: nc.alloc_sbuf_tensor_at(
        name, shape, dt, offset=region_addr + off).ap()
    phiq = at("phiq", 0, [128, H * 1024], BF16)
    phik = at("phik", 32768, [128, H * 1024], BF16)
    kv = at("kv", 65536, [128, H * 1024], BF16)
    scr0 = at("scr0", 98304, [128, 1024], F32)
    scr1 = at("scr1", 102400, [128, 1024], F32)
    scr2 = at("scr2", 106496, [128, 1024], F32)
    ao = at("ao", 0, [128, NT * 2048], BF16)
    racc = at("racc", 0, [128, NT * 2048], BF16)
    gs = at("gs", 32768, [128, NFI * 1024], BF16)
    gu = at("gu", 55296, [128, NFI * 1024], BF16)
    lg = at("lg", 103168, [128, 64], F32)
    exs = at("exs", 103424, [128, 64], F32)
    scs = at("scs", 103680, [128, 64], F32)

    scr3 = sbuf("scr3", [128, 1024], F32)
    pb = [nc.alloc_psum_tensor(f"pb{k}", [128, 512], F32).ap() for k in range(8)]

    with ExitStack() as stack:
        mk = lambda name: Ctr(stack.enter_context(nc.semaphore(name)))
        sems = dict(s_x=mk("s_x"), s_wf=mk("s_wf"),
                    s_mm=mk("s_mm"), s_ve=mk("s_ve"), s_ac=mk("s_ac"),
                    s_cc=mk("s_cc"), s_gt=mk("s_gt"), s_out=mk("s_out"),
                    s_gp=mk("s_gp"))
        B = Builder(nc, sems)
        B.s_wslot = [mk(f"s_wslot{k}") for k in range(RING)]
        s_xt = [mk(f"s_xt{j}") for j in range(NT)]
        s_x, s_wf = B.s_x, B.s_wf
        s_mm, s_ve, s_ac = B.s_mm, B.s_ve, B.s_ac
        s_cc, s_gt, s_out = B.s_cc, B.s_gt, B.s_out
        s_gp = B.s_gp

        x_ms = []
        for j in range(NT):
            x_ms.append(B.dma(r_bf[:, j * 2048:(j + 1) * 2048],
                              x_in[j * 128:(j + 1) * 128, :], s_xt[j]))
        B.dma(idn[:], idn_h[:], s_x)
        B.dma(mask[:], mask_h[:], s_x)
        B.dma(cvec[:], cvec_h[:], s_x)
        B.dma(wr_sb[:], wr_h[:], s_x)
        LOADS_ALL = (s_x, s_x.n)

        def tap(name, src_ap, ready_ms):
            if name not in taps:
                return
            B.wait("gpsimd", ready_ms)
            tms = B.dma(tap_outs[name], src_ap, s_out)
            B.wait("vector", tms)
            B.wait("scalar", tms)
            B.wait("tensor", tms)

        for _ in range(salt):
            B.emit("sync", lambda e: e.nop())
        B.op("vector", lambda e: e.memset(zeros[:], 0.0))
        B.wait("scalar", LOADS_ALL)
        B.wait("tensor", LOADS_ALL)

        def ln_group(src_fn, prewait_v_fn=None, prewait_s_fn=None,
                     tiles=range(NT), skip_mean=False, mean_from_accum=False):
            tiles = list(tiles)
            t0, tn = tiles[0], tiles[-1]
            sl8 = slice(t0, tn + 1)
            n8 = tn + 1 - t0
            if mean_from_accum:
                for j in tiles:
                    if prewait_v_fn is not None:
                        B.wait("vector", prewait_v_fn(j))
                B.opw("vector", lambda e: e.reduce_sum(
                    stats[:, BSUM + t0:BSUM + t0 + n8],
                    stats[:, 96 + 4 * t0:96 + 4 * (t0 + n8)].rearrange(
                        "p (j n) -> p j n", j=n8, n=4), axis=AX.X), s_ve)
            elif not skip_mean:
                for j in tiles:
                    if prewait_v_fn is not None:
                        B.wait("vector", prewait_v_fn(j))
                    B.op("vector", lambda e, s=src_fn(j), jj=j: e.reduce_sum(
                        stats[:, BSUM + jj:BSUM + jj + 1], s, axis=AX.X),
                        s_ve if j == tn else None)
                B.wait("vector", (s_ve, s_ve.n))
            for j in tiles:
                if prewait_s_fn is not None:
                    B.wait("scalar", prewait_s_fn(j))
                B.op("scalar", lambda e, s=src_fn(j), jj=j: e.activation(
                    scr0[:, 0:1024], s[:, 0:1024], AF.Square,
                    accum_out=stats[:, BSQ + jj:BSQ + jj + 1]))
                B.op("scalar", lambda e, s=src_fn(j), jj=j: e.activation(
                    scr1[:, 0:1024], s[:, 1024:2048], AF.Square,
                    accum_out=stats[:, BM2 + jj:BM2 + jj + 1]),
                    s_ac if j == tn else None)
            a_sq = (s_ac, s_ac.n)
            B.wait("scalar", a_sq)
            if mean_from_accum or not skip_mean:
                B.opw("vector", lambda e: e.tensor_scalar(
                    stats[:, BMEAN + t0:BMEAN + t0 + n8],
                    stats[:, BSUM + t0:BSUM + t0 + n8],
                    1.0 / D, None, ALU.mult), s_ve)
            B.wait("vector", a_sq)
            B.opw("vector", lambda e: e.tensor_tensor(
                stats[:, BSQ + t0:BSQ + t0 + n8],
                stats[:, BSQ + t0:BSQ + t0 + n8],
                stats[:, BM2 + t0:BM2 + t0 + n8], ALU.add), s_ve)
            B.opw("vector", lambda e: e.tensor_scalar(
                stats[:, BM2 + t0:BM2 + t0 + n8],
                stats[:, BSQ + t0:BSQ + t0 + n8],
                1.0 / D, None, ALU.mult), s_ve)
            B.opw("vector", lambda e: e.tensor_tensor(
                stats[:, BMM + t0:BMM + t0 + n8],
                stats[:, BMEAN + t0:BMEAN + t0 + n8],
                stats[:, BMEAN + t0:BMEAN + t0 + n8], ALU.mult), s_ve)
            v_var = B.opw("vector", lambda e: e.tensor_tensor(
                stats[:, BVAR + t0:BVAR + t0 + n8],
                stats[:, BM2 + t0:BM2 + t0 + n8],
                stats[:, BMM + t0:BMM + t0 + n8], ALU.subtract), s_ve)
            B.wait("scalar", v_var)
            B.opw("scalar", lambda e: e.activation(
                stats[:, BLNV + t0:BLNV + t0 + n8],
                stats[:, BVAR + t0:BVAR + t0 + n8],
                AF.Ln, bias=cvec[:, C_LNEPS:C_LNEPS + 1]), s_ac)
            a_rstd = B.opw("scalar", lambda e: e.activation(
                stats[:, BRSTD + t0:BRSTD + t0 + n8],
                stats[:, BLNV + t0:BLNV + t0 + n8],
                AF.Exp, scale=-0.5), s_ac)
            B.wait("vector", a_rstd)
            return a_rstd

        def ln_normalize(j, src, war_pe=None):
            B.wait("vector", war_pe)
            return B.op("vector", lambda e, s=src, jj=j: e.tensor_scalar(
                xn_tmp[:], s, stats[:, BMEAN + jj:BMEAN + jj + 1],
                stats[:, BRSTD + jj:BRSTD + jj + 1],
                ALU.subtract, ALU.mult), s_ve)

        def transpose16(dstT, j, norm_ms):
            last = None
            last_mm = None
            for g in range(4):
                bank = 6 + (g & 1)
                for m in range(4):
                    i = 4 * g + m
                    if m == 0:
                        B.wait("tensor", norm_ms if g == 0 else None)
                    last_mm = B.mm(bank, pb[bank][:, m * 128:(m + 1) * 128],
                                   xn_tmp[:, i * 128:(i + 1) * 128], idn[:],
                                   start=(m == 0), stop=(m == 3),
                                   ms=s_mm if m == 3 else None, first=(m == 0),
                                   sgc=True)
                dst = dstT.rearrange(
                    "p (i jj f) -> p i jj f", i=DC, jj=NT)[
                    :, 4 * g:4 * g + 4, j, :]
                last = B.evac("vector", lambda e, d=dst, b=bank:
                              e.tensor_copy(d, pb[b][:, 0:512]), bank, s_ve)
            return last, last_mm

        prev_tr_mm = None
        for half in range(2):
            ln_group(lambda j: r_bf[:, j * 2048:(j + 1) * 2048],
                     prewait_v_fn=lambda j: x_ms[j],
                     prewait_s_fn=lambda j: x_ms[j],
                     tiles=range(half * 4, half * 4 + 4))
            for j in range(half * 4, half * 4 + 4):
                nm = ln_normalize(j, r_bf[:, j * 2048:(j + 1) * 2048],
                                  war_pe=prev_tr_mm)
                _, prev_tr_mm = transpose16(arenaA, j, nm)
        xnT_ready = (s_ve, s_ve.n)
        tap("xnT", arenaA[:], xnT_ready)

        prev_ek = None
        prev_sttk = None
        B.wait("tensor", xnT_ready)
        s_done_ms = None
        bank_of = {"k": (2, 3), "q": (0, 1), "v": (4, 5)}

        def qkv_load(namec, h):
            c3 = {"q": 0, "k": 1, "v": 2}[namec]
            blk = c3 * 16 + h
            return B.wload(wqkv[blk * 128:(blk + 1) * 128, :], 2048, wring)

        def qkv_chains(namec, h, preloaded=None):
            idx, slot, ready = preloaded if preloaded else qkv_load(namec, h)
            B.wait("tensor", ready)
            last_ms = None
            for tc in range(2):
                bank = bank_of[namec][tc]
                for i in range(DC):
                    last_ms = B.mm(
                        bank, pb[bank][:],
                        slot[:, i * 128:(i + 1) * 128],
                        arenaA[:, i * 1024 + tc * 512:i * 1024 + tc * 512 + 512],
                        start=(i == 0), stop=(i == DC - 1),
                        ms=s_mm if i == DC - 1 else None, first=(i == 0))
            B.wfree(idx, last_ms)

        for h in range(H):
            qkv_chains("k", h)
            qkv_chains("v", h)
            ek_ms = None
            sttk_ms = None
            for tc in range(2):
                kb, vb = bank_of["k"][tc], bank_of["v"][tc]
                co = h * 1024 + tc * 512
                sl = slice(tc * 512, tc * 512 + 512)
                B.wait("vector", B.bank_w[kb])
                B.wait("vector", prev_ek)
                t0 = B.op("vector", lambda e, b=kb, s=sl: e.tensor_scalar(
                    scr0[:, s], pb[b][:], 0.0, None, ALU.min), s_ve)
                B.wait("scalar", t0)
                B.wait("scalar", prev_sttk)
                ek_ms = B.op("scalar", lambda e, s=sl: e.activation(
                    scr1[:, s], scr0[:, s], AF.Exp), s_ac)
                B.wait("vector", ek_ms)
                sttk_ms = B.evac("vector", lambda e, b=kb, s=sl, c=co, t=tc:
                                 e.scalar_tensor_tensor(
                                     phik[:, c:c + 512], pb[b][:], 0.0,
                                     scr1[:, s], ALU.max, ALU.add,
                                     accum_out=Stmp[:, t:t + 1]), kb, s_ve)
                B.evac("vector", lambda e, b=vb, c=co, t=tc:
                       e.scalar_tensor_tensor(
                           kv[:, c:c + 512], pb[b][:], 0.0,
                           phik[:, c:c + 512], ALU.add, ALU.mult,
                           accum_out=Stmp[:, 2 + t:3 + t]), vb, s_ve)
            prev_ek = ek_ms
            prev_sttk = sttk_ms
            B.wait("vector", (s_ve, s_ve.n))
            B.op("vector", lambda e, hh=h: e.tensor_tensor(
                S[:, 16 + hh:17 + hh], Stmp[:, 0:1], Stmp[:, 1:2], ALU.add))
            s_done_ms = B.op("vector", lambda e, hh=h: e.tensor_tensor(
                S[:, hh:hh + 1], Stmp[:, 2:3], Stmp[:, 3:4], ALU.add), s_ve)

        q_pre = [qkv_load("q", h) for h in range(RING)]

        tap("phik", phik[:], s_done_ms)
        tap("kvt", kv[:], s_done_ms)
        tap("phiq", phiq[:], s_done_ms)
        tap("Sp", S[:], s_done_ms)

        if fake_cc:
            B.wait("vector", s_done_ms)
            prefix_ms = B.op("vector", lambda e: e.memset(prefix[:], 0.0), s_ve)
        else:
            B.wait("gpsimd", s_done_ms)
            st_ms = B.dma(cc_in[:, :], S[:], s_gt)
            B.wait("gpsimd", st_ms)
            s_cc.n += 1
            B.emit("gpsimd", lambda e: e.collective_compute(
                "AllGather", ALU.bypass,
                replica_groups=[list(range(NCORE))],
                ins=[cc_in.ap().opt()], outs=[cc_out.ap().opt()],
            ).then_inc(s_cc.h, 1))
            cc_done = (s_cc, s_cc.n)
            prefix_ms = "deferred"

        prev_eq = None
        prev_sttq = None
        for h in range(H):
            qkv_chains("q", h, preloaded=q_pre[h] if h < len(q_pre) else None)
            eq_ms = None
            sttq_ms = None
            for tc in range(2):
                qb = bank_of["q"][tc]
                co = h * 1024 + tc * 512
                sl = slice(tc * 512, tc * 512 + 512)
                B.wait("vector", B.bank_w[qb])
                B.wait("vector", prev_eq if prev_eq is not None else prev_ek)
                t0q = B.op("vector", lambda e, b=qb, s=sl: e.tensor_scalar(
                    scr0[:, s], pb[b][:], 0.0, None, ALU.min), s_ve)
                B.wait("scalar", t0q)
                B.wait("scalar", prev_sttq if prev_sttq is not None
                       else prev_sttk)
                eq_ms = B.op("scalar", lambda e, s=sl: e.activation(
                    scr1[:, s], scr0[:, s], AF.Exp), s_ac)
                B.wait("vector", eq_ms)
                sttq_ms = B.evac("vector", lambda e, b=qb, s=sl, c=co:
                                 e.scalar_tensor_tensor(
                                     phiq[:, c:c + 512], pb[b][:], 0.0,
                                     scr1[:, s], ALU.max, ALU.add), qb, s_ve)
            prev_eq = eq_ms
            prev_sttq = sttq_ms

        if not fake_cc:
            B.wait("gpsimd", cc_done)
            g_ms = None
            for rk in range(NCORE):
                g_ms = B.dma(gath[:, rk * 32:(rk + 1) * 32],
                             cc_out[rk * 128:(rk + 1) * 128, :], s_gt)
            B.wait("vector", g_ms)
            B.wait("vector", LOADS_ALL)
            B.opw("vector", lambda e: e.memset(prefix[:], 0.0), s_ve)
            prefix_ms = None
            for rk in range(NCORE):
                prefix_ms = B.opw(
                    "vector", lambda e, r=rk: e.scalar_tensor_tensor(
                        prefix[:], gath[:, r * 32:(r + 1) * 32],
                        mask[:, r:r + 1], prefix[:], ALU.mult, ALU.add), s_ve)

        tap("pref", prefix[:], prefix_ms)

        amix_ms = []
        prev_rec = None
        B.wait("vector", prefix_ms)
        B.wait("vector", (s_ac, s_ac.n))
        B.wait("gpsimd", prefix_ms)
        ao_evac = {}
        wout_free = []

        def passB_head(h):
            nonlocal prev_rec
            co = h * 1024
            B.wait("vector", prev_rec)
            B.op("vector", lambda e, c=co, hh=h: e.tensor_tensor_scan(
                scr0[:], kv[:, c:c + 1024], zeros[:],
                prefix[:, hh:hh + 1], ALU.add, ALU.add))
            B.op("vector", lambda e, c=co, hh=h: e.tensor_tensor_scan(
                scr2[:], phik[:, c:c + 1024], zeros[:],
                prefix[:, 16 + hh:17 + hh], ALU.add, ALU.add))
            B.op("vector", lambda e, c=co: e.tensor_tensor(
                scr3[:], scr0[:], phiq[:, c:c + 1024], ALU.mult))
            den_ms = B.op("vector", lambda e, c=co: e.tensor_tensor(
                scr1[:], scr2[:], phiq[:, c:c + 1024], ALU.mult), s_ve)
            B.wait("scalar", den_ms)
            B.op("scalar", lambda e: e.activation(
                scr2[:], scr1[:], AF.Ln, bias=cvec[:, C_ATTEPS:C_ATTEPS + 1]))
            rec_ms = B.op("scalar", lambda e: e.activation(
                scr1[:], scr2[:], AF.Exp, scale=-1.0,
                bias=cvec[:, C_LNS:C_LNS + 1]), s_ac)
            B.wait("vector", rec_ms)
            amix_ms.append(B.op("vector", lambda e, c=co: e.tensor_tensor(
                arenaA[:, c:c + 1024], scr3[:], scr1[:], ALU.mult), s_ve))
            prev_rec = rec_ms

        def wout_pair(n, h2):
            blk = n * 16 + h2 * 2
            idx, slot, ready = B.wload(
                wout[blk * 128:(blk + 2) * 128, :].rearrange(
                    "(i p) f -> p i f", i=2, p=128), 1024, wring)
            last_ms = None
            for hi in range(2):
                hh = h2 * 2 + hi
                if n == 0:
                    B.wait("tensor", amix_ms[hh])
                if hi == 0:
                    B.wait("tensor", ready)
                for j in range(NT):
                    want_ms = (hh == H - 1)
                    m = B.mm(j, pb[j][:],
                             arenaA[:, hh * 1024 + j * 128:
                                    hh * 1024 + j * 128 + 128],
                             slot[:, hi * 512:(hi + 1) * 512],
                             start=(hh == 0), stop=(hh == H - 1),
                             ms=s_mm if want_ms else
                             (s_wf if (j == NT - 1 and hi == 1
                                       and hh < H - 1) else None),
                             first=(hh == 0))
                    if j == NT - 1 and hi == 1:
                        last_ms = m
            B.wfree(idx, last_ms)

        for h in range(H):
            passB_head(h)
            if h % 2 == 1:
                wout_pair(0, h // 2)
        for j in range(NT):
            ao_evac[(0, j)] = B.evac(
                "scalar", lambda e, jj=j: e.activation(
                    ao[:, jj * 2048:jj * 2048 + 512],
                    pb[jj][:], AF.Copy,
                    accum_out=stats[:, 96 + jj * 4:97 + jj * 4]), j, s_ac)
        for n in range(1, NDO):
            for h2 in range(H // 2):
                wout_pair(n, h2)
            for j in range(NT):
                ao_evac[(n, j)] = B.evac(
                    "scalar", lambda e, jj=j, nn=n: e.activation(
                        ao[:, jj * 2048 + nn * 512:jj * 2048 + nn * 512 + 512],
                        pb[jj][:], AF.Copy,
                        accum_out=stats[:, 96 + jj * 4 + nn:
                                        97 + jj * 4 + nn]), j, s_ac)
        B.opw("vector", lambda e: e.tensor_copy(
            stats[:, 80:88], stats[:, BMEAN:BMEAN + 8]), s_ve)
        ln_group(lambda j: ao[:, j * 2048:(j + 1) * 2048],
                 prewait_v_fn=lambda j: ao_evac[(NDO - 1, j)],
                 prewait_s_fn=lambda j: ao_evac[(NDO - 1, j)],
                 mean_from_accum=True)
        r1_ms = None
        radd_ms = []
        for j in range(NT):
            ln_normalize(j, ao[:, j * 2048:(j + 1) * 2048])
            r1_ms = B.op("vector", lambda e, jj=j: e.tensor_tensor(
                r_bf[:, jj * 2048:(jj + 1) * 2048],
                r_bf[:, jj * 2048:(jj + 1) * 2048], xn_tmp[:], ALU.add), s_ve)
            radd_ms.append(r1_ms)
        tap("amix", arenaA[:], amix_ms[-1])
        tap("r1", r_bf[:], r1_ms)
        tap("ao", ao[:], r1_ms)

        mm_after_wout = (s_mm, s_mm.n)
        B.opw("vector", lambda e: e.tensor_copy(
            stats[:, BMEAN:BMEAN + 8], stats[:, 80:88]), s_ve)
        ln_group(lambda j: r_bf[:, j * 2048:(j + 1) * 2048],
                 prewait_s_fn=lambda j: radd_ms[j], skip_mean=True)
        B.wait("tensor", mm_after_wout)
        prev_tr_mm = None
        for j in range(NT):
            nm = ln_normalize(j, r_bf[:, j * 2048:(j + 1) * 2048],
                              war_pe=prev_tr_mm)
            _, prev_tr_mm = transpose16(arenaA, j, nm)
        xn2T_ready = (s_ve, s_ve.n)
        tap("xn2T", arenaA[:], xn2T_ready)

        B.wait("tensor", xn2T_ready)
        rt_ms = None
        for j in range(NT):
            for i in range(DC):
                rt_ms = B.mm(6, pb[6][:, j * 8:j * 8 + 8],
                             arenaA[:, i * 1024 + j * 128:i * 1024 + j * 128 + 128],
                             wr_sb[:, i * 8:(i + 1) * 8],
                             start=(j == 0 and i == 0),
                             stop=(j == NT - 1 and i == DC - 1),
                             ms=s_mm if (j == NT - 1 and i == DC - 1) else None,
                             first=(j == 0 and i == 0), sgc=True)
        lg_ms = B.evac("vector", lambda e: e.tensor_copy(lg[:, 0:64],
                                                         pb[6][:, 0:64]), 6, s_ve)
        for j in range(NT):
            sl = slice(8 * j, 8 * j + 8)
            B.opw("vector", lambda e, s=sl: e.reduce_max(
                stats[:, ST_M1:ST_M1 + 1], lg[:, s], axis=AX.X), s_ve)
            nm1 = B.opw("vector", lambda e: e.tensor_scalar(
                stats[:, ST_NM1:ST_NM1 + 1], stats[:, ST_M1:ST_M1 + 1],
                -1.0, None, ALU.mult), s_ve)
            B.wait("scalar", nm1)
            ex_ms = B.opw("scalar", lambda e, s=sl: e.activation(
                exs[:, s], lg[:, s], AF.Exp,
                bias=stats[:, ST_NM1:ST_NM1 + 1]), s_ac)
            B.wait("vector", ex_ms)
            B.opw("vector", lambda e, s=sl: e.reduce_sum(
                stats[:, ST_SSUM:ST_SSUM + 1], exs[:, s], axis=AX.X), s_ve)
            B.opw("vector", lambda e: e.reciprocal(
                stats[:, ST_RSUM:ST_RSUM + 1],
                stats[:, ST_SSUM:ST_SSUM + 1]), s_ve)
            B.opw("vector", lambda e, s=sl: e.tensor_scalar(
                scs[:, s], exs[:, s], stats[:, ST_RSUM:ST_RSUM + 1],
                None, ALU.mult), s_ve)
            B.opw("vector", lambda e, s=sl: e.reduce_max(
                stats[:, ST_SM1:ST_SM1 + 1], scs[:, s], axis=AX.X), s_ve)
            B.opw("vector", lambda e, s=sl: e.tensor_scalar(
                exs[:, s], scs[:, s], stats[:, ST_SM1:ST_SM1 + 1],
                None, ALU.is_equal), s_ve)
            B.opw("vector", lambda e, s=sl: e.tensor_tensor(
                lg[:, s], scs[:, s], exs[:, s], ALU.subtract), s_ve)
            B.opw("vector", lambda e, s=sl: e.reduce_max(
                stats[:, ST_SM2:ST_SM2 + 1], lg[:, s], axis=AX.X), s_ve)
            B.opw("vector", lambda e, s=sl: e.tensor_scalar(
                exs[:, s], scs[:, s], stats[:, ST_SM2:ST_SM2 + 1],
                None, ALU.is_ge), s_ve)
            B.opw("vector", lambda e, s=sl: e.tensor_tensor(
                wfr[:, s], scs[:, s], exs[:, s], ALU.mult), s_ve)
        wf_ms = (s_ve, s_ve.n)
        tap("wf", wfr[:], wf_ms)

        racc_evac = {}
        gu_ms_prev = None
        for b in range(NBRANCH):
            eidx = b - 2
            gu_ms = {}
            for mat, wh in (("g", wg_h), ("u", wu_h)):
                for fi in range(NFI):
                    off = (b * NFI + fi) * DC * 128 * 128
                    qsl = wh[off:off + DC * 128 * 128].rearrange(
                        "(i p f) -> p i f", i=DC, p=128, f=128)
                    idx, slot, ready = B.wload(qsl, DC * 128, wring)
                    B.wait("tensor", ready)
                    for tc in range(2):
                        bank = (2 * fi + tc) % 8
                        for i in range(DC):
                            m = B.mm(
                                bank, pb[bank][:],
                                slot[:, i * 128:(i + 1) * 128],
                                arenaA[:, i * 1024 + tc * 512:
                                       i * 1024 + tc * 512 + 512],
                                start=(i == 0), stop=(i == DC - 1),
                                ms=s_mm if i == DC - 1 else None,
                                first=(i == 0))
                        if tc == 1:
                            B.wfree(idx, m)
                        dsl = slice(fi * 1024 + tc * 512,
                                    fi * 1024 + tc * 512 + 512)
                        if mat == "g":
                            if gu_ms_prev is not None:
                                B.wait("scalar", gu_ms_prev[fi])
                            B.evac("scalar", lambda e, d=dsl, bk=bank:
                                   e.activation(gs[:, d], pb[bk][:], AF.Silu),
                                   bank, s_ac)
                        else:
                            gu_ms[fi] = B.evac(
                                "vector", lambda e, d=dsl, bk=bank:
                                e.scalar_tensor_tensor(
                                    gu[:, d], pb[bk][:], 1.0, gs[:, d],
                                    ALU.mult, ALU.mult), bank, s_ve)
            gu_ms_prev = gu_ms
            for n in range(NDO):
                for f4 in range(3):
                    nfi = 4 if f4 < 2 else 3
                    rb = ((b * NDO + n) * NFI + f4 * 4) * 128
                    idx, slot, ready = B.wload(
                        wd_h[rb:rb + nfi * 128, :].rearrange(
                            "(i p) f -> p i f", i=nfi, p=128),
                        nfi * 512, wring)
                    B.wait("tensor", ready)
                    if n == 0:
                        B.wait("tensor", gu_ms[f4 * 4 + nfi - 1])
                    last_ms = None
                    for ii in range(nfi):
                        fi = f4 * 4 + ii
                        for j in range(NT):
                            is_end = (fi == NFI - 1)
                            m = B.mm(
                                j, pb[j][:],
                                gu[:, fi * 1024 + j * 128:
                                   fi * 1024 + j * 128 + 128],
                                slot[:, ii * 512:(ii + 1) * 512],
                                start=(fi == 0), stop=is_end,
                                ms=s_mm if is_end else
                                (s_wf if (ii == nfi - 1 and j == NT - 1
                                          and fi != NFI - 1) else None),
                                first=(fi == 0))
                            if ii == nfi - 1 and j == NT - 1:
                                last_ms = m
                    B.wfree(idx, last_ms)
                for j in range(NT):
                    dsl = slice(j * 2048 + n * 512, j * 2048 + n * 512 + 512)
                    gate = wfr[:, 8 * j + eidx:8 * j + eidx + 1]
                    if b == 0:
                        racc_evac[(n, j)] = B.evac(
                            "vector", lambda e, d=dsl, jj=j:
                            e.tensor_copy(racc[:, d], pb[jj][:]), j, s_ve)
                    elif b == 1:
                        racc_evac[(n, j)] = B.evac(
                            "vector", lambda e, d=dsl, jj=j: e.tensor_tensor(
                                racc[:, d], racc[:, d], pb[jj][:], ALU.add),
                            j, s_ve)
                    elif b < NBRANCH - 1:
                        racc_evac[(n, j)] = B.evac(
                            "vector", lambda e, d=dsl, jj=j, g=gate:
                            e.scalar_tensor_tensor(
                                racc[:, d], pb[jj][:], g, racc[:, d],
                                ALU.mult, ALU.add), j, s_ve)
                    else:
                        racc_evac[(n, j)] = B.evac(
                            "vector", lambda e, d=dsl, jj=j, nn=n, g=gate:
                            e.scalar_tensor_tensor(
                                racc[:, d], pb[jj][:], g, racc[:, d],
                                ALU.mult, ALU.add,
                                accum_out=stats[:, 96 + jj * 4 + nn:
                                                97 + jj * 4 + nn]), j, s_ve)
        tap("racc", racc[:], racc_evac[(NDO - 1, NT - 1)])

        for half in range(2):
            ln_group(lambda j: racc[:, j * 2048:(j + 1) * 2048],
                     prewait_v_fn=lambda j: racc_evac[(NDO - 1, j)],
                     prewait_s_fn=lambda j: racc_evac[(NDO - 1, j)],
                     tiles=range(half * 4, half * 4 + 4),
                     mean_from_accum=True)
            for j in range(half * 4, half * 4 + 4):
                ln_normalize(j, racc[:, j * 2048:(j + 1) * 2048])
                add_ms = B.op("vector", lambda e, jj=j: e.tensor_tensor(
                    r_bf[:, jj * 2048:(jj + 1) * 2048],
                    r_bf[:, jj * 2048:(jj + 1) * 2048], xn_tmp[:],
                    ALU.add), s_ve)
                B.wait("gpsimd", add_ms)
                B.dma(y_out[j * 128:(j + 1) * 128, :],
                      r_bf[:, j * 2048:(j + 1) * 2048], s_out)
        B.wait("gpsimd", (s_out, s_out.n))

        with nc.Block() as block:
            @block.sync
            def _(eng):
                for f in B.q["sync"]:
                    f(eng)

            @block.vector
            def _(eng):
                for f in B.q["vector"]:
                    f(eng)

            @block.scalar
            def _(eng):
                for f in B.q["scalar"]:
                    f(eng)

            @block.tensor
            def _(eng):
                for f in B.q["tensor"]:
                    f(eng)

            @block.gpsimd
            def _(eng):
                for f in B.q["gpsimd"]:
                    f(eng)

    return nc



def _prep_inputs(L0_attn_x, L0_attn_W_qkv, L0_attn_W_out, L0_ffn_Wr, L0_ffn_Wg,
                 L0_ffn_Wu, L0_ffn_Wd, L0_ffn_Wsg, L0_ffn_Wsu, L0_ffn_Wsd):
    f32 = np.float32
    asc = np.ascontiguousarray
    x = np.asarray(L0_attn_x, f32).reshape(8192, D)

    Wqkv = np.asarray(L0_attn_W_qkv, f32)
    wqkv_host = asc(Wqkv.reshape(3, 16, 128, 16, 128)
                    .transpose(0, 1, 4, 3, 2)).reshape(48 * 128, 2048)
    Wo = np.asarray(L0_attn_W_out, f32)
    wout_host = asc(Wo.reshape(4, 512, 16, 128)
                    .transpose(0, 2, 3, 1)).reshape(64 * 128, 512)
    Wr = np.asarray(L0_ffn_Wr, f32)
    wr_host = asc(Wr.reshape(8, 16, 128).transpose(2, 1, 0)).reshape(128, 128)

    Wsg = np.asarray(L0_ffn_Wsg, f32)
    Wsu = np.asarray(L0_ffn_Wsu, f32)
    Wsd = np.asarray(L0_ffn_Wsd, f32)
    Wg = np.asarray(L0_ffn_Wg, f32)
    Wu = np.asarray(L0_ffn_Wu, f32)
    Wd = np.asarray(L0_ffn_Wd, f32)

    up_g = [Wsg[0:FE], Wsg[FE:2 * FE]] + [Wg[e] for e in range(8)]
    up_u = [Wsu[0:FE], Wsu[FE:2 * FE]] + [Wu[e] for e in range(8)]
    dn = [Wsd[:, 0:FE], Wsd[:, FE:2 * FE]] + [Wd[e] for e in range(8)]

    def up_flat(mats):
        parts = []
        for M in mats:
            for fi in range(NFI):
                blk = (M[fi * 128:(fi + 1) * 128]
                       .reshape(128, 16, 128).transpose(1, 2, 0))
                parts.append(asc(blk).reshape(-1))
        return np.concatenate(parts)

    wg_host = up_flat(up_g)
    wu_host = up_flat(up_u)

    dparts = []
    for M in dn:
        for n in range(NDO):
            for fi in range(NFI):
                blk = M[n * 512:(n + 1) * 512, fi * 128:(fi + 1) * 128].T
                dparts.append(asc(blk))
    wd_host = np.concatenate(dparts, axis=0)

    import ml_dtypes
    idn = np.eye(128, dtype=ml_dtypes.bfloat16)
    cvec = np.zeros((128, 8), f32)
    cvec[:, C_LNEPS] = LN_EPS
    cvec[:, C_ATTEPS] = ATT_EPS
    cvec[:, C_LNS] = np.log(ATT_SCALE)

    in_maps = []
    for c in range(NCORE):
        m = np.zeros((128, 8), f32)
        m[:, :c] = 1.0
        in_maps.append(dict(
            x_in=asc(x[c * T:(c + 1) * T]), wqkv=wqkv_host, wout=wout_host,
            wr=wr_host, wg=wg_host, wu=wu_host, wd=wd_host, idn_in=idn,
            mask_in=m, cvec_in=cvec))
    return in_maps


def kernel(L0_attn_x, L0_attn_W_qkv, L0_attn_W_out, L0_ffn_Wr, L0_ffn_Wg,
           L0_ffn_Wu, L0_ffn_Wd, L0_ffn_Wsg, L0_ffn_Wsu, L0_ffn_Wsd):
    in_maps = _prep_inputs(L0_attn_x, L0_attn_W_qkv, L0_attn_W_out, L0_ffn_Wr,
                           L0_ffn_Wg, L0_ffn_Wu, L0_ffn_Wd, L0_ffn_Wsg,
                           L0_ffn_Wsu, L0_ffn_Wsd)
    nc = build()
    res = run_bass_kernel_spmd(nc, in_maps, core_ids=list(range(NCORE)))
    out = np.concatenate(
        [np.asarray(res.results[c]["y_out"], np.float32) for c in range(NCORE)],
        axis=0)
    return out.reshape(1, 8192, D)


# version 06
# speedup vs baseline: 1.2089x; 1.0592x over previous
import numpy as np
from contextlib import ExitStack

import concourse.bass as bass
import concourse.mybir as mybir
from concourse.bass_utils import run_bass_kernel_spmd

F32 = mybir.dt.float32
BF16 = mybir.dt.bfloat16
AF = mybir.ActivationFunctionType
ALU = mybir.AluOpType
AX = mybir.AxisListType

NCORE = 8
T = 1024
NT = 8
D = 2048
DC = 16
H = 16
HD = 128
FE = 1408
NBRANCH = 10
FCH = [(0, 512), (512, 512), (1024, 384)]
NFI = 11
NDO = 4
LN_EPS = 1e-5
ATT_EPS = 1e-9
ATT_SCALE = float(HD) ** -0.5
RING = 4
RING_COLS = 2048

C_LNEPS, C_ATTEPS, C_LNS = 0, 1, 2
(ST_SUM, ST_SQ, ST_MEAN, ST_M2, ST_MM, ST_VAR, ST_LNV, ST_RSTD,
 ST_M1, ST_NM1, ST_SSUM, ST_RSUM, ST_SM1, ST_SM2) = range(14)
BSUM, BSQ, BM2, BMEAN, BMM, BVAR, BLNV, BRSTD = (16, 24, 32, 40, 48, 56, 64, 72)


class Ctr:
    def __init__(self, handle):
        self.h = handle
        self.n = 0


class Builder:

    def __init__(self, nc, sems):
        self.nc = nc
        self.q = {e: [] for e in ("sync", "vector", "scalar", "tensor", "gpsimd")}
        for name, ctr in sems.items():
            setattr(self, name, ctr)
        self.bank_w = [None] * 8
        self.bank_r = [None] * 8
        self.widx = 0
        self.wfree_ms = {}

    def emit(self, eng, fn):
        self.q[eng].append(fn)

    def wait(self, eng, ms):
        if ms is None:
            return
        ctr, val = ms
        assert val <= ctr.n, f"wait on future milestone {val} > {ctr.n}"
        if val > 0:
            self.q[eng].append(lambda e, c=ctr, v=val: e.wait_ge(c.h, v))

    def op(self, eng, fn, ms=None):
        if ms is None:
            self.q[eng].append(fn)
            return None
        ms.n += 1
        v = ms.n
        self.q[eng].append(lambda e, f=fn, c=ms: f(e).then_inc(c.h, 1))
        return (ms, v)

    def opw(self, eng, fn, ms):
        r = self.op(eng, fn, ms)
        self.wait(eng, r)
        return r

    def dma(self, out, in_, ctr, eng="gpsimd"):
        ctr.n += 16
        v = ctr.n
        self.q[eng].append(lambda e, o=out, i=in_, c=ctr: e.dma_start(
            out=o, in_=i).then_inc(c.h, 16))
        return (ctr, v)

    def wload(self, in_ap, ncols, wring):
        idx = self.widx
        self.widx += 1
        k = idx % RING
        slot = wring[k][:, 0:ncols]
        if idx >= RING:
            self.wait("gpsimd", self.wfree_ms.pop(idx - RING))
        dst = slot
        if len(in_ap.shape) == 3:
            dst = slot.rearrange("p (i f) -> p i f", i=in_ap.shape[1],
                                 f=in_ap.shape[2])
        ready = self.dma(dst, in_ap, self.s_wslot[k])
        return idx, slot, ready

    def wfree(self, idx, ms):
        assert idx not in self.wfree_ms
        self.wfree_ms[idx] = ms

    def mm(self, bank, out_ap, lhsT, rhs, start, stop, ms=None, first=False,
           sgc=False):
        if first and self.bank_r[bank] is not None:
            self.wait("tensor", self.bank_r[bank])
        r = self.op("tensor", lambda e, o=out_ap, l=lhsT, rr=rhs, st=start,
                    sp=stop, g=sgc: e.matmul(o, l, rr, start=st, stop=sp,
                                             skip_group_check=g), ms)
        if stop and ms is not None:
            self.bank_w[bank] = r
        return r

    def phase_wait(self, banks=range(8)):
        best = {}
        for b in banks:
            ms = self.bank_r[b]
            if ms is None:
                continue
            ctr, val = ms
            if val > best.get(id(ctr), (ctr, 0))[1]:
                best[id(ctr)] = (ctr, val)
        for ctr, val in best.values():
            self.wait("tensor", (ctr, val))

    def evac(self, eng, fn, bank, ms_ctr):
        assert self.bank_w[bank] is not None
        self.wait(eng, self.bank_w[bank])
        r = self.op(eng, fn, ms_ctr)
        self.bank_r[bank] = r
        return r


def build(fake_cc=False, taps=(), salt=0):
    nc = bass.Bass(trn_type="TRN2", target_bir_lowering=False,
                   detect_race_conditions=False)

    x_in = nc.dram_tensor("x_in", [T, D], F32, kind="ExternalInput").ap()
    wqkv = nc.dram_tensor("wqkv", [48 * 128, 2048], F32, kind="ExternalInput").ap()
    wout = nc.dram_tensor("wout", [64 * 128, 512], F32, kind="ExternalInput").ap()
    wr_h = nc.dram_tensor("wr", [128, 128], F32, kind="ExternalInput").ap()
    wg_h = nc.dram_tensor("wg", [NBRANCH * FE * D], F32, kind="ExternalInput").ap()
    wu_h = nc.dram_tensor("wu", [NBRANCH * FE * D], F32, kind="ExternalInput").ap()
    wd_h = nc.dram_tensor("wd", [NBRANCH * NDO * NFI * 128, 512], F32,
                          kind="ExternalInput").ap()
    idn_h = nc.dram_tensor("idn_in", [128, 128], BF16, kind="ExternalInput").ap()
    mask_h = nc.dram_tensor("mask_in", [128, 8], F32, kind="ExternalInput").ap()
    cvec_h = nc.dram_tensor("cvec_in", [128, 8], F32, kind="ExternalInput").ap()
    y_out = nc.dram_tensor("y_out", [T, D], F32, kind="ExternalOutput").ap()
    cc_in = nc.dram_tensor("cc_in", [128, 32], F32, kind="Internal")
    cc_out = nc.dram_tensor("cc_out", [128 * NCORE, 32], F32, kind="Internal")

    tap_outs = {}
    for tname in taps:
        shp = {"xnT": [128, DC * 1024], "amix": [128, H * 1024],
               "r1": [128, NT * 2048], "wf": [128, 64],
               "racc": [128, NT * 2048], "xn2T": [128, DC * 1024],
               "phik": [128, H * 1024], "kvt": [128, H * 1024],
               "Sp": [128, 32], "pref": [128, 32],
               "phiq": [128, H * 1024], "gs": [128, NT * FE],
               "ao": [128, NT * 2048]}[tname]
        tap_outs[tname] = nc.dram_tensor("tap_" + tname, shp, F32,
                                         kind="ExternalOutput").ap()

    sbuf = lambda name, shape, dt: nc.alloc_sbuf_tensor(name, shape, dt).ap()
    r_bf = sbuf("r_bf", [128, NT * 2048], BF16)
    arenaA = sbuf("arenaA", [128, 16384], BF16)
    zeros = sbuf("zeros", [128, 1024], BF16)
    idn = sbuf("idn", [128, 128], BF16)
    wr_sb = sbuf("wr_sb", [128, 128], BF16)
    mask = sbuf("mask", [128, 8], F32)
    cvec = sbuf("cvec", [128, 8], F32)
    S = sbuf("S", [128, 32], F32)
    Stmp = sbuf("Stmp", [128, 8], F32)
    gath = sbuf("gath", [128, 256], F32)
    prefix = sbuf("prefix", [128, 32], F32)
    wfr = sbuf("wfr", [128, 64], F32)
    stats = sbuf("stats", [128, 128], F32)
    xn_tmp_h = nc.alloc_sbuf_tensor("xn_tmp", [128, 2048], BF16)
    xn_tmp = xn_tmp_h.ap()
    scr4 = sbuf("scr4", [128, 1024], F32)
    scr5 = nc.alloc_sbuf_tensor_at(
        "scr5", [128, 1024], F32,
        offset=nc.lookup_mls(xn_tmp_h).memorylocations[0].addr).ap()
    wring = [sbuf(f"wring{k}", [128, RING_COLS], BF16) for k in range(RING)]

    region_h = nc.alloc_sbuf_tensor("region", [128, 55296], BF16)
    region_addr = nc.lookup_mls(region_h).memorylocations[0].addr
    at = lambda name, off, shape, dt: nc.alloc_sbuf_tensor_at(
        name, shape, dt, offset=region_addr + off).ap()
    phiq = at("phiq", 0, [128, H * 1024], BF16)
    phik = at("phik", 32768, [128, H * 1024], BF16)
    kv = at("kv", 65536, [128, H * 1024], BF16)
    scr0 = at("scr0", 98304, [128, 1024], F32)
    scr1 = at("scr1", 102400, [128, 1024], F32)
    scr2 = at("scr2", 106496, [128, 1024], F32)
    ao = at("ao", 0, [128, NT * 2048], BF16)
    racc = at("racc", 0, [128, NT * 2048], BF16)
    gs = at("gs", 32768, [128, NFI * 1024], BF16)
    gu = at("gu", 55296, [128, NFI * 1024], BF16)
    wd9 = at("wd9", 77824, [128, 2 * NFI * 512], BF16)
    lg = at("lg", 103168, [128, 64], F32)
    exs = at("exs", 103424, [128, 64], F32)
    scs = at("scs", 103680, [128, 64], F32)

    scr3 = sbuf("scr3", [128, 1024], F32)
    pb = [nc.alloc_psum_tensor(f"pb{k}", [128, 512], F32).ap() for k in range(8)]

    with ExitStack() as stack:
        mk = lambda name: Ctr(stack.enter_context(nc.semaphore(name)))
        sems = dict(s_x=mk("s_x"), s_wf=mk("s_wf"),
                    s_mm=mk("s_mm"), s_ve=mk("s_ve"), s_ac=mk("s_ac"),
                    s_cc=mk("s_cc"), s_gt=mk("s_gt"), s_out=mk("s_out"),
                    s_gp=mk("s_gp"))
        B = Builder(nc, sems)
        B.s_wslot = [mk(f"s_wslot{k}") for k in range(RING)]
        s_xt = [mk(f"s_xt{j}") for j in range(NT)]
        s_wd9 = mk("s_wd9")
        s_x, s_wf = B.s_x, B.s_wf
        s_mm, s_ve, s_ac = B.s_mm, B.s_ve, B.s_ac
        s_cc, s_gt, s_out = B.s_cc, B.s_gt, B.s_out
        s_gp = B.s_gp

        x_ms = []
        for j in range(NT):
            x_ms.append(B.dma(r_bf[:, j * 2048:(j + 1) * 2048],
                              x_in[j * 128:(j + 1) * 128, :], s_xt[j]))
        B.dma(idn[:], idn_h[:], s_x)
        B.dma(mask[:], mask_h[:], s_x)
        B.dma(cvec[:], cvec_h[:], s_x)
        B.dma(wr_sb[:], wr_h[:], s_x)
        LOADS_ALL = (s_x, s_x.n)

        def tap(name, src_ap, ready_ms):
            if name not in taps:
                return
            B.wait("gpsimd", ready_ms)
            tms = B.dma(tap_outs[name], src_ap, s_out)
            B.wait("vector", tms)
            B.wait("scalar", tms)
            B.wait("tensor", tms)

        for _ in range(salt):
            B.emit("sync", lambda e: e.nop())
        B.op("vector", lambda e: e.memset(zeros[:], 0.0))
        B.wait("scalar", LOADS_ALL)
        B.wait("tensor", LOADS_ALL)

        def ln_group(src_fn, prewait_v_fn=None, prewait_s_fn=None,
                     tiles=range(NT), skip_mean=False, mean_from_accum=False):
            tiles = list(tiles)
            t0, tn = tiles[0], tiles[-1]
            sl8 = slice(t0, tn + 1)
            n8 = tn + 1 - t0
            if mean_from_accum:
                for j in tiles:
                    if prewait_v_fn is not None:
                        B.wait("vector", prewait_v_fn(j))
                B.opw("vector", lambda e: e.reduce_sum(
                    stats[:, BSUM + t0:BSUM + t0 + n8],
                    stats[:, 96 + 4 * t0:96 + 4 * (t0 + n8)].rearrange(
                        "p (j n) -> p j n", j=n8, n=4), axis=AX.X), s_ve)
            elif not skip_mean:
                for j in tiles:
                    if prewait_v_fn is not None:
                        B.wait("vector", prewait_v_fn(j))
                    B.op("vector", lambda e, s=src_fn(j), jj=j: e.reduce_sum(
                        stats[:, BSUM + jj:BSUM + jj + 1], s, axis=AX.X),
                        s_ve if j == tn else None)
                B.wait("vector", (s_ve, s_ve.n))
            for j in tiles:
                if prewait_s_fn is not None:
                    B.wait("scalar", prewait_s_fn(j))
                B.op("scalar", lambda e, s=src_fn(j), jj=j: e.activation(
                    scr2[:, 0:1024], s[:, 0:1024], AF.Square,
                    accum_out=stats[:, BSQ + jj:BSQ + jj + 1]))
                B.op("scalar", lambda e, s=src_fn(j), jj=j: e.activation(
                    scr3[:, 0:1024], s[:, 1024:2048], AF.Square,
                    accum_out=stats[:, BM2 + jj:BM2 + jj + 1]),
                    s_ac if j == tn else None)
            a_sq = (s_ac, s_ac.n)
            B.wait("scalar", a_sq)
            if mean_from_accum or not skip_mean:
                B.opw("vector", lambda e: e.tensor_scalar(
                    stats[:, BMEAN + t0:BMEAN + t0 + n8],
                    stats[:, BSUM + t0:BSUM + t0 + n8],
                    1.0 / D, None, ALU.mult), s_ve)
            B.wait("vector", a_sq)
            B.opw("vector", lambda e: e.tensor_tensor(
                stats[:, BSQ + t0:BSQ + t0 + n8],
                stats[:, BSQ + t0:BSQ + t0 + n8],
                stats[:, BM2 + t0:BM2 + t0 + n8], ALU.add), s_ve)
            B.opw("vector", lambda e: e.tensor_scalar(
                stats[:, BM2 + t0:BM2 + t0 + n8],
                stats[:, BSQ + t0:BSQ + t0 + n8],
                1.0 / D, None, ALU.mult), s_ve)
            B.opw("vector", lambda e: e.tensor_tensor(
                stats[:, BMM + t0:BMM + t0 + n8],
                stats[:, BMEAN + t0:BMEAN + t0 + n8],
                stats[:, BMEAN + t0:BMEAN + t0 + n8], ALU.mult), s_ve)
            v_var = B.opw("vector", lambda e: e.tensor_tensor(
                stats[:, BVAR + t0:BVAR + t0 + n8],
                stats[:, BM2 + t0:BM2 + t0 + n8],
                stats[:, BMM + t0:BMM + t0 + n8], ALU.subtract), s_ve)
            B.wait("scalar", v_var)
            B.opw("scalar", lambda e: e.activation(
                stats[:, BLNV + t0:BLNV + t0 + n8],
                stats[:, BVAR + t0:BVAR + t0 + n8],
                AF.Ln, bias=cvec[:, C_LNEPS:C_LNEPS + 1]), s_ac)
            a_rstd = B.opw("scalar", lambda e: e.activation(
                stats[:, BRSTD + t0:BRSTD + t0 + n8],
                stats[:, BLNV + t0:BLNV + t0 + n8],
                AF.Exp, scale=-0.5), s_ac)
            B.wait("vector", a_rstd)
            return a_rstd

        def ln_normalize(j, src, war_pe=None):
            B.wait("vector", war_pe)
            return B.op("vector", lambda e, s=src, jj=j: e.tensor_scalar(
                xn_tmp[:], s, stats[:, BMEAN + jj:BMEAN + jj + 1],
                stats[:, BRSTD + jj:BRSTD + jj + 1],
                ALU.subtract, ALU.mult), s_ve)

        def transpose16(dstT, j, norm_ms):
            last = None
            last_mm = None
            for g in range(4):
                bank = 6 + (g & 1)
                for m in range(4):
                    i = 4 * g + m
                    if m == 0:
                        B.wait("tensor", norm_ms if g == 0 else None)
                    last_mm = B.mm(bank, pb[bank][:, m * 128:(m + 1) * 128],
                                   xn_tmp[:, i * 128:(i + 1) * 128], idn[:],
                                   start=(m == 0), stop=(m == 3),
                                   ms=s_mm if m == 3 else None, first=(m == 0),
                                   sgc=True)
                dst = dstT.rearrange(
                    "p (i jj f) -> p i jj f", i=DC, jj=NT)[
                    :, 4 * g:4 * g + 4, j, :]
                last = B.evac("vector", lambda e, d=dst, b=bank:
                              e.tensor_copy(d, pb[b][:, 0:512]), bank, s_ve)
            return last, last_mm

        prev_tr_mm = None
        for half in range(2):
            ln_group(lambda j: r_bf[:, j * 2048:(j + 1) * 2048],
                     prewait_v_fn=lambda j: x_ms[j],
                     prewait_s_fn=lambda j: x_ms[j],
                     tiles=range(half * 4, half * 4 + 4))
            for j in range(half * 4, half * 4 + 4):
                nm = ln_normalize(j, r_bf[:, j * 2048:(j + 1) * 2048],
                                  war_pe=prev_tr_mm)
                _, prev_tr_mm = transpose16(arenaA, j, nm)
        xnT_ready = (s_ve, s_ve.n)
        tap("xnT", arenaA[:], xnT_ready)

        prev_ek = None
        prev_sttk = None
        B.wait("tensor", xnT_ready)
        s_done_ms = None
        bank_of = {"k": (2, 3), "q": (0, 1), "v": (4, 5)}

        def qkv_load(namec, h):
            c3 = {"q": 0, "k": 1, "v": 2}[namec]
            blk = c3 * 16 + h
            return B.wload(wqkv[blk * 128:(blk + 1) * 128, :], 2048, wring)

        def qkv_chains(namec, h, preloaded=None):
            idx, slot, ready = preloaded if preloaded else qkv_load(namec, h)
            B.wait("tensor", ready)
            last_ms = None
            for tc in range(2):
                bank = bank_of[namec][tc]
                for i in range(DC):
                    last_ms = B.mm(
                        bank, pb[bank][:],
                        slot[:, i * 128:(i + 1) * 128],
                        arenaA[:, i * 1024 + tc * 512:i * 1024 + tc * 512 + 512],
                        start=(i == 0), stop=(i == DC - 1),
                        ms=s_mm if i == DC - 1 else None, first=(i == 0))
            B.wfree(idx, last_ms)

        for h in range(H):
            qkv_chains("k", h)
            qkv_chains("v", h)
            ek_ms = None
            sttk_ms = None
            for tc in range(2):
                kb, vb = bank_of["k"][tc], bank_of["v"][tc]
                co = h * 1024 + tc * 512
                sl = slice(tc * 512, tc * 512 + 512)
                B.wait("vector", B.bank_w[kb])
                B.wait("vector", prev_ek)
                t0 = B.op("vector", lambda e, b=kb, s=sl: e.tensor_scalar(
                    scr0[:, s], pb[b][:], 0.0, None, ALU.min), s_ve)
                B.wait("scalar", t0)
                B.wait("scalar", prev_sttk)
                ek_ms = B.op("scalar", lambda e, s=sl: e.activation(
                    scr1[:, s], scr0[:, s], AF.Exp), s_ac)
                B.wait("vector", ek_ms)
                sttk_ms = B.evac("vector", lambda e, b=kb, s=sl, c=co, t=tc:
                                 e.scalar_tensor_tensor(
                                     phik[:, c:c + 512], pb[b][:], 0.0,
                                     scr1[:, s], ALU.max, ALU.add,
                                     accum_out=Stmp[:, t:t + 1]), kb, s_ve)
                B.evac("vector", lambda e, b=vb, c=co, t=tc:
                       e.scalar_tensor_tensor(
                           kv[:, c:c + 512], pb[b][:], 0.0,
                           phik[:, c:c + 512], ALU.add, ALU.mult,
                           accum_out=Stmp[:, 2 + t:3 + t]), vb, s_ve)
            prev_ek = ek_ms
            prev_sttk = sttk_ms
            B.wait("vector", (s_ve, s_ve.n))
            B.op("vector", lambda e, hh=h: e.tensor_tensor(
                S[:, 16 + hh:17 + hh], Stmp[:, 0:1], Stmp[:, 1:2], ALU.add))
            s_done_ms = B.op("vector", lambda e, hh=h: e.tensor_tensor(
                S[:, hh:hh + 1], Stmp[:, 2:3], Stmp[:, 3:4], ALU.add), s_ve)

        q_pre = [qkv_load("q", h) for h in range(RING)]

        tap("phik", phik[:], s_done_ms)
        tap("kvt", kv[:], s_done_ms)
        tap("phiq", phiq[:], s_done_ms)
        tap("Sp", S[:], s_done_ms)

        if fake_cc:
            B.wait("vector", s_done_ms)
            prefix_ms = B.op("vector", lambda e: e.memset(prefix[:], 0.0), s_ve)
        else:
            B.wait("gpsimd", s_done_ms)
            st_ms = B.dma(cc_in[:, :], S[:], s_gt)
            B.wait("gpsimd", st_ms)
            s_cc.n += 1
            B.emit("gpsimd", lambda e: e.collective_compute(
                "AllGather", ALU.bypass,
                replica_groups=[list(range(NCORE))],
                ins=[cc_in.ap().opt()], outs=[cc_out.ap().opt()],
            ).then_inc(s_cc.h, 1))
            cc_done = (s_cc, s_cc.n)
            prefix_ms = "deferred"

        prev_eq = None
        prev_sttq = None
        for h in range(H):
            qkv_chains("q", h, preloaded=q_pre[h] if h < len(q_pre) else None)
            eq_ms = None
            sttq_ms = None
            for tc in range(2):
                qb = bank_of["q"][tc]
                co = h * 1024 + tc * 512
                sl = slice(tc * 512, tc * 512 + 512)
                B.wait("vector", B.bank_w[qb])
                B.wait("vector", prev_eq if prev_eq is not None else prev_ek)
                t0q = B.op("vector", lambda e, b=qb, s=sl: e.tensor_scalar(
                    scr0[:, s], pb[b][:], 0.0, None, ALU.min), s_ve)
                B.wait("scalar", t0q)
                B.wait("scalar", prev_sttq if prev_sttq is not None
                       else prev_sttk)
                eq_ms = B.op("scalar", lambda e, s=sl: e.activation(
                    scr1[:, s], scr0[:, s], AF.Exp), s_ac)
                B.wait("vector", eq_ms)
                sttq_ms = B.evac("vector", lambda e, b=qb, s=sl, c=co:
                                 e.scalar_tensor_tensor(
                                     phiq[:, c:c + 512], pb[b][:], 0.0,
                                     scr1[:, s], ALU.max, ALU.add), qb, s_ve)
            prev_eq = eq_ms
            prev_sttq = sttq_ms

        if not fake_cc:
            B.wait("gpsimd", cc_done)
            g_ms = None
            for rk in range(NCORE):
                g_ms = B.dma(gath[:, rk * 32:(rk + 1) * 32],
                             cc_out[rk * 128:(rk + 1) * 128, :], s_gt)
            B.wait("vector", g_ms)
            B.wait("vector", LOADS_ALL)
            B.opw("vector", lambda e: e.memset(prefix[:], 0.0), s_ve)
            prefix_ms = None
            for rk in range(NCORE):
                prefix_ms = B.opw(
                    "vector", lambda e, r=rk: e.scalar_tensor_tensor(
                        prefix[:], gath[:, r * 32:(r + 1) * 32],
                        mask[:, r:r + 1], prefix[:], ALU.mult, ALU.add), s_ve)

        tap("pref", prefix[:], prefix_ms)

        amix_ms = []
        prev_rec = None
        B.wait("vector", prefix_ms)
        B.wait("vector", (s_ac, s_ac.n))
        B.wait("gpsimd", prefix_ms)
        ao_evac = {}
        wout_free = []

        rec = {}

        def passB_head(h):
            p = h % 2
            kcs = scr2 if p == 0 else scr5
            dens = scr1 if p == 0 else scr4
            co = h * 1024
            if h - 2 in rec:
                B.wait("vector", rec[h - 2])
            B.op("vector", lambda e, c=co, hh=h: e.tensor_tensor_scan(
                scr0[:], kv[:, c:c + 1024], zeros[:],
                prefix[:, hh:hh + 1], ALU.add, ALU.add))
            B.op("vector", lambda e, c=co, hh=h, d=kcs: e.tensor_tensor_scan(
                d[:], phik[:, c:c + 1024], zeros[:],
                prefix[:, 16 + hh:17 + hh], ALU.add, ALU.add))
            den_ms = B.op("vector", lambda e, c=co, a=kcs, d=dens:
                          e.tensor_tensor(d[:], a[:], phiq[:, c:c + 1024],
                                          ALU.mult), s_ve)
            B.wait("scalar", den_ms)
            B.op("scalar", lambda e, a=dens, d=kcs: e.activation(
                d[:], a[:], AF.Ln, bias=cvec[:, C_ATTEPS:C_ATTEPS + 1]))
            rec[h] = B.op("scalar", lambda e, a=kcs, d=dens: e.activation(
                d[:], a[:], AF.Exp, scale=-1.0,
                bias=cvec[:, C_LNS:C_LNS + 1]), s_ac)

        def emit_num(h):
            co = h * 1024
            B.op("vector", lambda e, c=co: e.tensor_tensor(
                scr3[:], scr0[:], phiq[:, c:c + 1024], ALU.mult))

        def emit_amix(h):
            dens = scr1 if h % 2 == 0 else scr4
            co = h * 1024
            B.wait("vector", rec[h])
            amix_ms.append(B.op("vector", lambda e, c=co, b2=dens:
                           e.tensor_tensor(arenaA[:, c:c + 1024], scr3[:],
                                           b2[:], ALU.mult), s_ve))

        def wout_pair(n, h2):
            blk = n * 16 + h2 * 2
            idx, slot, ready = B.wload(
                wout[blk * 128:(blk + 2) * 128, :].rearrange(
                    "(i p) f -> p i f", i=2, p=128), 1024, wring)
            last_ms = None
            for hi in range(2):
                hh = h2 * 2 + hi
                if n == 0:
                    B.wait("tensor", amix_ms[hh])
                if hi == 0:
                    B.wait("tensor", ready)
                for j in range(NT):
                    want_ms = (hh == H - 1)
                    m = B.mm(j, pb[j][:],
                             arenaA[:, hh * 1024 + j * 128:
                                    hh * 1024 + j * 128 + 128],
                             slot[:, hi * 512:(hi + 1) * 512],
                             start=(hh == 0), stop=(hh == H - 1),
                             ms=s_mm if want_ms else
                             (s_wf if (j == NT - 1 and hi == 1
                                       and hh < H - 1) else None),
                             first=(hh == 0))
                    if j == NT - 1 and hi == 1:
                        last_ms = m
            B.wfree(idx, last_ms)

        for h in range(H):
            passB_head(h)
            if h >= 1:
                emit_amix(h - 1)
            emit_num(h)
            if h >= 3 and h % 2 == 1:
                wout_pair(0, (h - 3) // 2)
        emit_amix(H - 1)
        wout_pair(0, H // 2 - 1)
        for j in range(NT):
            ao_evac[(0, j)] = B.evac(
                "scalar", lambda e, jj=j: e.activation(
                    ao[:, jj * 2048:jj * 2048 + 512],
                    pb[jj][:], AF.Copy,
                    accum_out=stats[:, 96 + jj * 4:97 + jj * 4]), j, s_ac)
        for n in range(1, NDO):
            for h2 in range(H // 2):
                wout_pair(n, h2)
            for j in range(NT):
                ao_evac[(n, j)] = B.evac(
                    "scalar", lambda e, jj=j, nn=n: e.activation(
                        ao[:, jj * 2048 + nn * 512:jj * 2048 + nn * 512 + 512],
                        pb[jj][:], AF.Copy,
                        accum_out=stats[:, 96 + jj * 4 + nn:
                                        97 + jj * 4 + nn]), j, s_ac)
        mm_after_wout = (s_mm, s_mm.n)
        r1_ms = None
        radd_ms = [None] * NT
        prev_tr_mm = None
        B.wait("tensor", mm_after_wout)
        for half in range(2):
            t0h = half * 4
            B.opw("vector", lambda e, t=t0h: e.tensor_copy(
                stats[:, 80 + t:84 + t], stats[:, BMEAN + t:BMEAN + t + 4]),
                s_ve)
            ln_group(lambda j: ao[:, j * 2048:(j + 1) * 2048],
                     prewait_v_fn=lambda j: ao_evac[(NDO - 1, j)],
                     prewait_s_fn=lambda j: ao_evac[(NDO - 1, j)],
                     tiles=range(t0h, t0h + 4), mean_from_accum=True)
            for j in range(t0h, t0h + 4):
                ln_normalize(j, ao[:, j * 2048:(j + 1) * 2048],
                             war_pe=prev_tr_mm if j == t0h else None)
                r1_ms = B.op("vector", lambda e, jj=j: e.tensor_tensor(
                    r_bf[:, jj * 2048:(jj + 1) * 2048],
                    r_bf[:, jj * 2048:(jj + 1) * 2048], xn_tmp[:], ALU.add),
                    s_ve)
                radd_ms[j] = r1_ms
            B.opw("vector", lambda e, t=t0h: e.tensor_copy(
                stats[:, BMEAN + t:BMEAN + t + 4], stats[:, 80 + t:84 + t]),
                s_ve)
            ln_group(lambda j: r_bf[:, j * 2048:(j + 1) * 2048],
                     prewait_s_fn=lambda j: radd_ms[j],
                     tiles=range(t0h, t0h + 4), skip_mean=True)
            for j in range(t0h, t0h + 4):
                nm = ln_normalize(j, r_bf[:, j * 2048:(j + 1) * 2048],
                                  war_pe=prev_tr_mm)
                _, prev_tr_mm = transpose16(arenaA, j, nm)
        tap("amix", arenaA[:], amix_ms[-1])
        tap("r1", r_bf[:], r1_ms)
        tap("ao", ao[:], r1_ms)
        xn2T_ready = (s_ve, s_ve.n)
        tap("xn2T", arenaA[:], xn2T_ready)

        B.wait("tensor", xn2T_ready)
        rt_ms = None
        for j in range(NT):
            for i in range(DC):
                rt_ms = B.mm(6, pb[6][:, j * 8:j * 8 + 8],
                             arenaA[:, i * 1024 + j * 128:i * 1024 + j * 128 + 128],
                             wr_sb[:, i * 8:(i + 1) * 8],
                             start=(j == 0 and i == 0),
                             stop=(j == NT - 1 and i == DC - 1),
                             ms=s_mm if (j == NT - 1 and i == DC - 1) else None,
                             first=(j == 0 and i == 0), sgc=True)
        lg_ms = B.evac("vector", lambda e: e.tensor_copy(lg[:, 0:64],
                                                         pb[6][:, 0:64]), 6, s_ve)
        for j in range(NT):
            sl = slice(8 * j, 8 * j + 8)
            B.opw("vector", lambda e, s=sl: e.reduce_max(
                stats[:, ST_M1:ST_M1 + 1], lg[:, s], axis=AX.X), s_ve)
            nm1 = B.opw("vector", lambda e: e.tensor_scalar(
                stats[:, ST_NM1:ST_NM1 + 1], stats[:, ST_M1:ST_M1 + 1],
                -1.0, None, ALU.mult), s_ve)
            B.wait("scalar", nm1)
            ex_ms = B.opw("scalar", lambda e, s=sl: e.activation(
                exs[:, s], lg[:, s], AF.Exp,
                bias=stats[:, ST_NM1:ST_NM1 + 1]), s_ac)
            B.wait("vector", ex_ms)
            B.opw("vector", lambda e, s=sl: e.reduce_sum(
                stats[:, ST_SSUM:ST_SSUM + 1], exs[:, s], axis=AX.X), s_ve)
            B.opw("vector", lambda e: e.reciprocal(
                stats[:, ST_RSUM:ST_RSUM + 1],
                stats[:, ST_SSUM:ST_SSUM + 1]), s_ve)
            B.opw("vector", lambda e, s=sl: e.tensor_scalar(
                scs[:, s], exs[:, s], stats[:, ST_RSUM:ST_RSUM + 1],
                None, ALU.mult), s_ve)
            B.opw("vector", lambda e, s=sl: e.reduce_max(
                stats[:, ST_SM1:ST_SM1 + 1], scs[:, s], axis=AX.X), s_ve)
            B.opw("vector", lambda e, s=sl: e.tensor_scalar(
                exs[:, s], scs[:, s], stats[:, ST_SM1:ST_SM1 + 1],
                None, ALU.is_equal), s_ve)
            B.opw("vector", lambda e, s=sl: e.tensor_tensor(
                lg[:, s], scs[:, s], exs[:, s], ALU.subtract), s_ve)
            B.opw("vector", lambda e, s=sl: e.reduce_max(
                stats[:, ST_SM2:ST_SM2 + 1], lg[:, s], axis=AX.X), s_ve)
            B.opw("vector", lambda e, s=sl: e.tensor_scalar(
                exs[:, s], scs[:, s], stats[:, ST_SM2:ST_SM2 + 1],
                None, ALU.is_ge), s_ve)
            B.opw("vector", lambda e, s=sl: e.tensor_tensor(
                wfr[:, s], scs[:, s], exs[:, s], ALU.mult), s_ve)
        wf_ms = (s_ve, s_ve.n)
        tap("wf", wfr[:], wf_ms)

        racc_evac = {}
        gu_ms_prev = None
        for b in range(NBRANCH):
            eidx = b - 2
            gu_ms = {}
            for mat, wh in (("g", wg_h), ("u", wu_h)):
                for fi in range(NFI):
                    off = (b * NFI + fi) * DC * 128 * 128
                    qsl = wh[off:off + DC * 128 * 128].rearrange(
                        "(i p f) -> p i f", i=DC, p=128, f=128)
                    idx, slot, ready = B.wload(qsl, DC * 128, wring)
                    B.wait("tensor", ready)
                    for tc in range(2):
                        bank = (2 * fi + tc) % 8
                        for i in range(DC):
                            m = B.mm(
                                bank, pb[bank][:],
                                slot[:, i * 128:(i + 1) * 128],
                                arenaA[:, i * 1024 + tc * 512:
                                       i * 1024 + tc * 512 + 512],
                                start=(i == 0), stop=(i == DC - 1),
                                ms=s_mm if i == DC - 1 else None,
                                first=(i == 0))
                        if tc == 1:
                            B.wfree(idx, m)
                        dsl = slice(fi * 1024 + tc * 512,
                                    fi * 1024 + tc * 512 + 512)
                        if mat == "g":
                            if gu_ms_prev is not None:
                                B.wait("scalar", gu_ms_prev[fi])
                            B.evac("scalar", lambda e, d=dsl, bk=bank:
                                   e.activation(gs[:, d], pb[bk][:], AF.Silu),
                                   bank, s_ac)
                        else:
                            gu_ms[fi] = B.evac(
                                "vector", lambda e, d=dsl, bk=bank:
                                e.scalar_tensor_tensor(
                                    gu[:, d], pb[bk][:], 1.0, gs[:, d],
                                    ALU.mult, ALU.mult), bank, s_ve)
            gu_ms_prev = gu_ms

            def racc_ev(n, j, bank):
                dsl = slice(j * 2048 + n * 512, j * 2048 + n * 512 + 512)
                gate = wfr[:, 8 * j + eidx:8 * j + eidx + 1]
                if b == 0:
                    racc_evac[(n, j)] = B.evac(
                        "vector", lambda e, d=dsl, bk=bank:
                        e.tensor_copy(racc[:, d], pb[bk][:]), bank, s_ve)
                elif b == 1:
                    racc_evac[(n, j)] = B.evac(
                        "vector", lambda e, d=dsl, bk=bank: e.tensor_tensor(
                            racc[:, d], racc[:, d], pb[bk][:], ALU.add),
                        bank, s_ve)
                elif b < NBRANCH - 1:
                    racc_evac[(n, j)] = B.evac(
                        "vector", lambda e, d=dsl, bk=bank, g=gate:
                        e.scalar_tensor_tensor(
                            racc[:, d], pb[bk][:], g, racc[:, d],
                            ALU.mult, ALU.add), bank, s_ve)
                else:
                    racc_evac[(n, j)] = B.evac(
                        "vector", lambda e, d=dsl, bk=bank, nn=n, g=gate:
                        e.scalar_tensor_tensor(
                            racc[:, d], pb[bk][:], g, racc[:, d],
                            ALU.mult, ALU.add,
                            accum_out=stats[:, 96 + 4 * j + nn:
                                            97 + 4 * j + nn]), bank, s_ve)

            last_b = (b == NBRANCH - 1)
            n_ring = (2, 3) if last_b else range(NDO)
            first_n = 2 if last_b else 0
            for n in n_ring:
                if b == NBRANCH - 2 and n >= 2:
                    n9 = n - 2
                    r0 = ((NBRANCH - 1) * NDO + n9) * NFI * 128
                    wd9_ms = B.dma(
                        wd9[:, n9 * NFI * 512:(n9 + 1) * NFI * 512].rearrange(
                            "p (i f) -> p i f", i=NFI, f=512),
                        wd_h[r0:r0 + NFI * 128, :].rearrange(
                            "(i p) f -> p i f", i=NFI, p=128), s_wd9)
                for f4 in range(3):
                    nfi = 4 if f4 < 2 else 3
                    rb = ((b * NDO + n) * NFI + f4 * 4) * 128
                    idx, slot, ready = B.wload(
                        wd_h[rb:rb + nfi * 128, :].rearrange(
                            "(i p) f -> p i f", i=nfi, p=128),
                        nfi * 512, wring)
                    B.wait("tensor", ready)
                    if n == first_n:
                        B.wait("tensor", gu_ms[f4 * 4 + nfi - 1])
                    last_ms = None
                    for ii in range(nfi):
                        fi = f4 * 4 + ii
                        for j in range(NT):
                            is_end = (fi == NFI - 1)
                            m = B.mm(
                                j, pb[j][:],
                                gu[:, fi * 1024 + j * 128:
                                   fi * 1024 + j * 128 + 128],
                                slot[:, ii * 512:(ii + 1) * 512],
                                start=(fi == 0), stop=is_end,
                                ms=s_mm if is_end else
                                (s_wf if (ii == nfi - 1 and j == NT - 1
                                          and fi != NFI - 1) else None),
                                first=(fi == 0))
                            if ii == nfi - 1 and j == NT - 1:
                                last_ms = m
                    B.wfree(idx, last_ms)
                for j in range(NT):
                    racc_ev(n, j, j)
            if last_b:
                B.wait("tensor", wd9_ms)
                for j in range(NT):
                    for n in (0, 1):
                        bank = (j % 4) * 2 + n
                        for fi in range(NFI):
                            B.mm(bank, pb[bank][:],
                                 gu[:, fi * 1024 + j * 128:
                                    fi * 1024 + j * 128 + 128],
                                 wd9[:, (n * NFI + fi) * 512:
                                     (n * NFI + fi + 1) * 512],
                                 start=(fi == 0), stop=(fi == NFI - 1),
                                 ms=s_mm if fi == NFI - 1 else None,
                                 first=(fi == 0))
                    for n in (0, 1):
                        racc_ev(n, j, (j % 4) * 2 + n)
                    ln_group(lambda jj: racc[:, jj * 2048:(jj + 1) * 2048],
                             prewait_v_fn=lambda jj: racc_evac[(1, jj)],
                             prewait_s_fn=lambda jj: racc_evac[(1, jj)],
                             tiles=[j], mean_from_accum=True)
                    ln_normalize(j, racc[:, j * 2048:(j + 1) * 2048])
                    add_ms = B.op("vector", lambda e, jj=j: e.tensor_tensor(
                        r_bf[:, jj * 2048:(jj + 1) * 2048],
                        r_bf[:, jj * 2048:(jj + 1) * 2048], xn_tmp[:],
                        ALU.add), s_ve)
                    B.wait("gpsimd", add_ms)
                    B.dma(y_out[j * 128:(j + 1) * 128, :],
                          r_bf[:, j * 2048:(j + 1) * 2048], s_out)
        tap("racc", racc[:], racc_evac[(1, NT - 1)])
        B.wait("gpsimd", (s_out, s_out.n))

        with nc.Block() as block:
            @block.sync
            def _(eng):
                for f in B.q["sync"]:
                    f(eng)

            @block.vector
            def _(eng):
                for f in B.q["vector"]:
                    f(eng)

            @block.scalar
            def _(eng):
                for f in B.q["scalar"]:
                    f(eng)

            @block.tensor
            def _(eng):
                for f in B.q["tensor"]:
                    f(eng)

            @block.gpsimd
            def _(eng):
                for f in B.q["gpsimd"]:
                    f(eng)

    return nc



def _prep_inputs(L0_attn_x, L0_attn_W_qkv, L0_attn_W_out, L0_ffn_Wr, L0_ffn_Wg,
                 L0_ffn_Wu, L0_ffn_Wd, L0_ffn_Wsg, L0_ffn_Wsu, L0_ffn_Wsd):
    f32 = np.float32
    asc = np.ascontiguousarray
    x = np.asarray(L0_attn_x, f32).reshape(8192, D)

    Wqkv = np.asarray(L0_attn_W_qkv, f32)
    wqkv_host = asc(Wqkv.reshape(3, 16, 128, 16, 128)
                    .transpose(0, 1, 4, 3, 2)).reshape(48 * 128, 2048)
    Wo = np.asarray(L0_attn_W_out, f32)
    wout_host = asc(Wo.reshape(4, 512, 16, 128)
                    .transpose(0, 2, 3, 1)).reshape(64 * 128, 512)
    Wr = np.asarray(L0_ffn_Wr, f32)
    wr_host = asc(Wr.reshape(8, 16, 128).transpose(2, 1, 0)).reshape(128, 128)

    Wsg = np.asarray(L0_ffn_Wsg, f32)
    Wsu = np.asarray(L0_ffn_Wsu, f32)
    Wsd = np.asarray(L0_ffn_Wsd, f32)
    Wg = np.asarray(L0_ffn_Wg, f32)
    Wu = np.asarray(L0_ffn_Wu, f32)
    Wd = np.asarray(L0_ffn_Wd, f32)

    up_g = [Wsg[0:FE], Wsg[FE:2 * FE]] + [Wg[e] for e in range(8)]
    up_u = [Wsu[0:FE], Wsu[FE:2 * FE]] + [Wu[e] for e in range(8)]
    dn = [Wsd[:, 0:FE], Wsd[:, FE:2 * FE]] + [Wd[e] for e in range(8)]

    def up_flat(mats):
        parts = []
        for M in mats:
            for fi in range(NFI):
                blk = (M[fi * 128:(fi + 1) * 128]
                       .reshape(128, 16, 128).transpose(1, 2, 0))
                parts.append(asc(blk).reshape(-1))
        return np.concatenate(parts)

    wg_host = up_flat(up_g)
    wu_host = up_flat(up_u)

    dparts = []
    for M in dn:
        for n in range(NDO):
            for fi in range(NFI):
                blk = M[n * 512:(n + 1) * 512, fi * 128:(fi + 1) * 128].T
                dparts.append(asc(blk))
    wd_host = np.concatenate(dparts, axis=0)

    import ml_dtypes
    idn = np.eye(128, dtype=ml_dtypes.bfloat16)
    cvec = np.zeros((128, 8), f32)
    cvec[:, C_LNEPS] = LN_EPS
    cvec[:, C_ATTEPS] = ATT_EPS
    cvec[:, C_LNS] = np.log(ATT_SCALE)

    in_maps = []
    for c in range(NCORE):
        m = np.zeros((128, 8), f32)
        m[:, :c] = 1.0
        in_maps.append(dict(
            x_in=asc(x[c * T:(c + 1) * T]), wqkv=wqkv_host, wout=wout_host,
            wr=wr_host, wg=wg_host, wu=wu_host, wd=wd_host, idn_in=idn,
            mask_in=m, cvec_in=cvec))
    return in_maps


def kernel(L0_attn_x, L0_attn_W_qkv, L0_attn_W_out, L0_ffn_Wr, L0_ffn_Wg,
           L0_ffn_Wu, L0_ffn_Wd, L0_ffn_Wsg, L0_ffn_Wsu, L0_ffn_Wsd):
    in_maps = _prep_inputs(L0_attn_x, L0_attn_W_qkv, L0_attn_W_out, L0_ffn_Wr,
                           L0_ffn_Wg, L0_ffn_Wu, L0_ffn_Wd, L0_ffn_Wsg,
                           L0_ffn_Wsu, L0_ffn_Wsd)
    nc = build()
    res = run_bass_kernel_spmd(nc, in_maps, core_ids=list(range(NCORE)))
    out = np.concatenate(
        [np.asarray(res.results[c]["y_out"], np.float32) for c in range(NCORE)],
        axis=0)
    return out.reshape(1, 8192, D)
